# version 05
# baseline (speedup 1.0000x reference)
import numpy as np
import concourse.bass as bass
import concourse.mybir as mybir

F32 = mybir.dt.float32
BF16 = mybir.dt.bfloat16
AF = mybir.ActivationFunctionType
OP = mybir.AluOpType
AX = mybir.AxisListType

D = 1024
KT = 8
FT = 16
NE = 8
NMOD = 48
LOG1EM7 = float(np.float32(np.log(np.float32(1e-7))))

ENGS = ("t", "v", "s", "g", "y")


class Sem:
    def __init__(self, h):
        self.h = h
        self.c = 0


class Sched:
    def __init__(self):
        self.q = {e: [] for e in ENGS}

    def op(self, eng, build, inc=None, n=1):
        if inc is not None:
            inc.c += n
            h = inc.h

            def f(e, build=build, h=h, n=n):
                build(e).then_inc(h, n)
            self.q[eng].append(f)
            return inc.c
        self.q[eng].append(lambda e, build=build: build(e))
        return None

    def wait(self, eng, sem, val):
        assert val is not None and val <= sem.c, (eng, val, sem.c)
        if val > 0:
            self.q[eng].append(lambda e, h=sem.h, v=val: e.wait_ge(h, v))

    def drain(self, eng):
        self.q[eng].append(lambda e: e.drain())


def host_prep(inputs, n_cores=8, L=2048):
    A = np.ascontiguousarray
    x = np.asarray(inputs["L0_attn_x"], dtype=np.float32)
    conv_w = np.asarray(inputs["L0_attn_conv_weight"], np.float32)
    conv_b = np.asarray(inputs["L0_attn_conv_bias"], np.float32)
    wr = np.asarray(inputs["L0_attn_Wr_w"], np.float32)
    wrb = np.asarray(inputs["L0_attn_Wr_b"], np.float32)
    wi = np.asarray(inputs["L0_attn_Wi_w"], np.float32)
    wib = np.asarray(inputs["L0_attn_Wi_b"], np.float32)
    log_a = np.asarray(inputs["L0_attn_log_a"], np.float32)
    ow = np.asarray(inputs["L0_attn_out_w"], np.float32)
    nw = np.asarray(inputs["L0_attn_norm_w"], np.float32)
    gw = np.asarray(inputs["L0_ffn_gate_w"], np.float32)
    w1 = np.asarray(inputs["L0_ffn_w1"], np.float32)
    w2 = np.asarray(inputs["L0_ffn_w2"], np.float32)
    w3 = np.asarray(inputs["L0_ffn_w3"], np.float32)
    c = np.asarray(inputs["L0_c"], np.float32)
    wmod = np.asarray(inputs["L0_wmod"], np.float32)

    def vec8(v):
        return A(v.reshape(8, 128).T)

    def sq_t(w):
        return A(w.T.reshape(8, 128, 8, 128).transpose(2, 1, 0, 3))

    def ffn_t(w):
        return w.transpose(0, 2, 1).reshape(8, 8, 128, 16, 128).transpose(0, 3, 2, 1, 4)

    w13 = A(np.stack([ffn_t(w1), ffn_t(w3)], axis=3))
    w2t = A(w2.transpose(0, 2, 1).reshape(8, 16, 128, 8, 128).transpose(0, 3, 2, 1, 4))
    wmod_t = A(wmod.reshape(8, 128, 12, 512))
    gwt = A(gw.T.reshape(8, 128, 8))

    convd = np.zeros((8, 5, 128, 128), np.float32)
    ii = np.arange(128)
    for ct in range(8):
        for k in range(4):
            convd[ct, k, ii, ii] = conv_w[ct * 128:(ct + 1) * 128, 0, k]
        convd[ct, 4, ii, ii] = conv_b[ct * 128:(ct + 1) * 128]
    shared = {
        "convd": A(convd.transpose(0, 2, 1, 3)),
        "convw": A(conv_w[:, 0, :].reshape(8, 128, 4).transpose(1, 0, 2).reshape(128, 32)),
        "convb": vec8(conv_b), "wrb": vec8(wrb), "wib": vec8(wib),
        "loga": vec8(log_a), "normw": vec8(nw),
        "wr": sq_t(wr), "wi": sq_t(wi), "ow": sq_t(ow),
        "gwt": gwt, "w13": w13, "w2t": w2t, "wmod": wmod_t,
        "eye": np.eye(128, dtype=np.float32),
        "sel8": np.ascontiguousarray(np.broadcast_to(
            np.eye(8, dtype=np.float32)[:, :, None], (8, 8, 128))),
    }
    xt = x.transpose(0, 2, 1)
    in_maps = []
    for b in range(n_cores):
        m = dict(shared)
        m["x"] = A(xt[b].reshape(8, 128, L))
        m["cvec"] = vec8(c[b])
        in_maps.append(m)
    return in_maps


def host_post(results):
    outs = []
    for r in results:
        y = np.asarray(r["y"])
        outs.append(np.ascontiguousarray(y.reshape(1024, -1).T))
    return np.stack(outs, axis=0)


def build(L=2048, taps=(), phase_limit=99):
    taps = set(taps)
    nc = bass.Bass("TRN2", target_bir_lowering=False)

    SCW = min(512, L)
    NSC = L // SCW
    TCW = min(1024, L)
    NTC = L // TCW
    CCW = min(512, TCW)
    NCC2 = TCW // CCW
    TT = L // 128

    dr = lambda name, shape: nc.dram_tensor(name, shape, F32, kind="ExternalInput").ap()
    x_d = dr("x", [KT, 128, L])
    cvec_d = dr("cvec", [128, 8])
    convw_d = dr("convw", [128, 32])
    convd_d = dr("convd", [8, 128, 5, 128])
    convb_d = dr("convb", [128, 8])
    wrb_d = dr("wrb", [128, 8])
    wib_d = dr("wib", [128, 8])
    loga_d = dr("loga", [128, 8])
    normw_d = dr("normw", [128, 8])
    wr_d = dr("wr", [8, 128, 8, 128])
    wi_d = dr("wi", [8, 128, 8, 128])
    ow_d = dr("ow", [8, 128, 8, 128])
    gwt_d = dr("gwt", [8, 128, 8])
    w13_d = dr("w13", [NE, FT, 128, 2, 8, 128])
    w2t_d = dr("w2t", [NE, 8, 128, 16, 128])
    wmod_d = dr("wmod", [8, 128, 12, 512])
    eye_d = dr("eye", [128, 128])
    sel8_d = dr("sel8", [8, 8, 128])
    y_d = nc.dram_tensor("y", [KT, 128, L], F32, kind="ExternalOutput").ap()
    tap_d = {}
    for t in taps:
        shp = {"probs": [128, TT * 8], "wsel": [128, TT * 8],
               "mod": [128, 56], "rstd": [128, L], "mean": [128, L]}.get(t, [KT, 128, L])
        tap_d[t] = nc.dram_tensor("tap_" + t, shp, F32, kind="ExternalOutput").ap()

    sch = Sched()

    def alloc(name, shape, dt=F32):
        cm = nc.sbuf_tensor(name, shape, dt)
        return cm.__enter__(), cm

    def palloc(name, shape):
        cm = nc.psum_tensor(name, shape, F32)
        return cm.__enter__(), cm

    sem_cms = []

    def mksem(name):
        cm = nc.semaphore(name)
        sem_cms.append(cm)
        return Sem(cm.__enter__())

    sX = mksem("sX")
    sXV = mksem("sXV")
    sGW = mksem("sGW")
    sEY = mksem("sEY")
    sWM = [mksem("sWM0"), mksem("sWM1")]
    sWS = [mksem(f"sWS{i}") for i in range(3)]
    sCV = [mksem("sCV0"), mksem("sCV1")]
    sW13 = [mksem("sW13a"), mksem("sW13b")]
    sW2 = [mksem("sW2a"), mksem("sW2b")]
    sO = mksem("sO")
    sT = mksem("sT")
    sV = mksem("sV")
    sS = mksem("sS")

    def tap_sync():
        sch.wait("y", sO, sO.c)
        sch.wait("v", sO, sO.c)
        sch.wait("s", sO, sO.c)

    def _replay():
        with nc.Block() as block:
            block.tensor(lambda e: [f(e) for f in sch.q["t"]])
            block.vector(lambda e: [f(e) for f in sch.q["v"]])
            block.scalar(lambda e: [f(e) for f in sch.q["s"]])
            block.gpsimd(lambda e: [f(e) for f in sch.q["g"]])
            block.sync(lambda e: [f(e) for f in sch.q["y"]])
        return nc

    def _finish_early():
        sch.wait("y", sV, sV.c)
        sch.wait("y", sS, sS.c)
        sch.wait("y", sT, sT.c)
        for ct in range(KT):
            sch.op("y", lambda e, ct=ct: e.dma_start(
                out=y_d[ct], in_=x32[:, ct * L:(ct + 1) * L]), inc=sO, n=16)
        sch.wait("y", sO, sO.c)
        return _replay()

    x32, _ = alloc("x32", [128, KT * L])
    vecs, _ = alloc("vecs", [128, 72])
    modv, _ = alloc("modv", [128, 56])
    cv32, _ = alloc("cv32", [128, 8])
    sc8, _ = alloc("sc8", [128, 32])
    ones_m, _ = alloc("ones_m", [128, 1], BF16)
    ones_r, _ = alloc("ones_r", [1, 128], BF16)
    gw_s, _ = alloc("gw_s", [128, 64], BF16)
    cst, _ = alloc("cst", [128, 7])

    wT_s, _ = alloc("wT_s", [8, L])
    hbf, cm_hb = alloc("hb", [128, KT * L], BF16)
    hffn = hbf
    meanB, cm_meanB = alloc("meanB", [128, L])
    rstdB, cm_rstdB = alloc("rstdB", [128, L])
    rwA, cm_rwA = alloc("rwA", [1, SCW])
    rwB, cm_rwB = alloc("rwB", [1, SCW])
    rwbf, cm_rwbf = alloc("rwbf", [1, 4 * SCW], BF16)
    chA, cm_chA = alloc("chA", [128, 4 * SCW], BF16)
    wmb, cm_wmb = alloc("wmb", [128, 2 * 3072])
    mrow, cm_mrow = alloc("mrow", [1, 3072])

    PS0, _ = palloc("PS0", [128, 2048])
    PS1, _ = palloc("PS1", [128, 2048])

    def EP(i):
        return E[:, i * L:(i + 1) * L]

    def col(t, j):
        return t[:, j:j + 1]

    V_CONVB, V_WRB, V_WIB, V_LOGA, V_NORMW, V_CONVW = 0, 8, 16, 24, 32, 40
    M_SA, M_CA1, M_GAC, M_SB, M_CB1, M_GBC, M_LA8 = 0, 8, 16, 24, 32, 40, 48

    for k in range(KT):
        sch.op("y", lambda e, k=k: e.dma_start(out=x32[:, k * L:(k + 1) * L], in_=x_d[k]),
               inc=sX, n=16)
    x_loaded = sX.c
    for dst, src in ((cv32[:], cvec_d),
                     (vecs[:, 0:8], convb_d), (vecs[:, 8:16], wrb_d),
                     (vecs[:, 16:24], wib_d), (vecs[:, 24:32], loga_d),
                     (vecs[:, 32:40], normw_d), (vecs[:, 40:72], convw_d)):
        sch.op("y", lambda e, dst=dst, src=src: e.dma_start(out=dst, in_=src), inc=sXV, n=16)
    vecs_ld = sXV.c
    sch.op("g", lambda e: e.dma_start(
        out=gw_s[:].rearrange("p (k e) -> p k e", k=8),
        in_=gwt_d.rearrange("k p e -> p k e")), inc=sGW, n=16)
    gw_ld = sGW.c
    wm_ld = {}
    wm_used = {}

    def issue_wm(half, k):
        prevk = (half, k - 2) if k >= 2 else ((half - 1, KT - 2 + k) if half > 0 else None)
        if prevk is not None and prevk in wm_used:
            sch.wait("g", sT, wm_used[prevk])
        sch.op("g", lambda e, h=half, k=k: e.dma_start(
            out=wmb[:, (k % 2) * 3072:(k % 2) * 3072 + 3072],
            in_=wmod_d[k][:, h * 6:(h + 1) * 6, :].rearrange("p a b -> p (a b)")),
            inc=sWM[k % 2], n=16)
        wm_ld[(half, k)] = (sWM[k % 2], sWM[k % 2].c)
    issue_wm(0, 0)
    issue_wm(0, 1)

    sch.op("v", lambda e: e.memset(ones_m[:], 1.0 / 1024.0))
    sch.op("v", lambda e: e.memset(ones_r[:], 1.0))
    sch.op("v", lambda e: e.memset(cst[:, 0:1], 1e-5))
    sch.op("v", lambda e: e.memset(cst[:, 1:2], 1e-6))
    sch.op("v", lambda e: e.memset(cst[:, 2:3], -0.5))
    sch.op("v", lambda e: e.memset(cst[:, 3:4], -1.0))
    sch.op("v", lambda e: e.memset(cst[:, 4:5], 1.0))
    sch.op("v", lambda e: e.memset(cst[:, 5:6], float(np.float32(1.0) - np.float32(1e-6))))
    v_cst = sch.op("v", lambda e: e.memset(
        cst[:, 6:7], float(np.float32(1.0) - (np.float32(1.0) - np.float32(1e-6)))), inc=sV)
    sch.drain("v")

    sch.wait("s", sXV, vecs_ld)
    sch.wait("s", sV, v_cst)
    s_silu = sch.op("s", lambda e: e.activation(cv32[:], cv32[:], AF.Silu), inc=sS)

    sch.drain("s")
    sch.op("s", lambda e: e.activation(sc8[:, 8:16], vecs[:, 24:32], AF.Sigmoid))
    sch.drain("s")
    s_ln = sch.op("s", lambda e: e.activation(sc8[:, 8:16], sc8[:, 8:16], AF.Ln), inc=sS)
    sch.wait("v", sS, s_ln)
    sch.drain("v")
    v_la8 = sch.op("v", lambda e: e.tensor_scalar(
        modv[:, M_LA8:M_LA8 + 8], sc8[:, 8:16], 8.0, None, OP.mult), inc=sV)

    sch.wait("t", sS, s_silu)
    mod_bank_free = [0, 0]
    mrow_free = 0
    tr_idx = 0
    for half in range(2):
        t_row = None
        for k in range(KT):
            sch.wait("t", *wm_ld[(half, k)])
            if k == 0 and mrow_free:
                sch.wait("t", sV, mrow_free)
            for nc_ in range(6):
                ps = PS0 if nc_ < 4 else PS1
                t_row = sch.op("t", lambda e, k=k, nc_=nc_, ps=ps: e.matmul(
                    ps[0:1, (nc_ % 4) * 512:(nc_ % 4) * 512 + 512],
                    cv32[:, k:k + 1],
                    wmb[:, (k % 2) * 3072 + nc_ * 512:(k % 2) * 3072 + nc_ * 512 + 512],
                    start=(k == 0), stop=(k == KT - 1)),
                    inc=sT if nc_ == 5 else None)
            wm_used[(half, k)] = t_row
            nk = (half, k + 2) if k + 2 < KT else ((half + 1, k + 2 - KT) if half == 0 else None)
            if nk is not None:
                issue_wm(*nk)
        sch.wait("v", sT, t_row)
        v_row = None
        for nc_ in range(6):
            ps = PS0 if nc_ < 4 else PS1
            v_row = sch.op("v", lambda e, nc_=nc_, ps=ps: e.tensor_copy(
                mrow[0:1, nc_ * 512:(nc_ + 1) * 512],
                ps[0:1, (nc_ % 4) * 512:(nc_ % 4) * 512 + 512]), inc=sV)
        mrow_free = v_row
        sch.wait("t", sV, v_row)
        if mod_bank_free[half % 2]:
            sch.wait("t", sV, mod_bank_free[half % 2])
        t_tr = None
        for blk in range(24):
            t_tr = sch.op("t", lambda e, blk=blk, half=half: e.matmul(
                PS1[:, (2 + half) * 512 + blk:(2 + half) * 512 + blk + 1],
                mrow[0:1, blk * 128:(blk + 1) * 128],
                cst[0:1, 4:5],
                start=(blk == 0), stop=True,
                skip_group_check=True), inc=sT if blk == 23 else None)
        sch.wait("v", sT, t_tr)
        mod_bank_free[half % 2] = sch.op(
            "v", lambda e, half=half: e.tensor_copy(
                modv[:, half * 24:(half + 1) * 24],
                PS1[:, (2 + half) * 512:(2 + half) * 512 + 24]), inc=sV)
    mod_done = sV.c
    mod_t_done = sT.c

    sch.drain("v")
    sch.op("v", lambda e: e.tensor_scalar(modv[:, 8:16], modv[:, 8:16], 1.0, None, OP.add))
    sch.op("v", lambda e: e.tensor_scalar(modv[:, 32:40], modv[:, 32:40], 1.0, None, OP.add))
    for GC in (M_GAC, M_GBC):
        sch.drain("v")
        sch.op("v", lambda e, G=GC: e.tensor_scalar(sc8[:, 0:8], modv[:, G:G + 8], 0.0, None, OP.is_ge))
        sch.op("v", lambda e, G=GC: e.tensor_scalar(sc8[:, 8:16], modv[:, G:G + 8], 0.25, None, OP.max))
        sch.op("v", lambda e, G=GC: e.tensor_scalar(sc8[:, 16:24], modv[:, G:G + 8], -0.25, None, OP.min))
        sch.drain("v")
        sch.op("v", lambda e: e.tensor_tensor(sc8[:, 8:16], sc8[:, 8:16], sc8[:, 16:24], OP.subtract))
        sch.drain("v")
        sch.op("v", lambda e: e.tensor_tensor(sc8[:, 8:16], sc8[:, 8:16], sc8[:, 0:8], OP.mult))
        sch.drain("v")
        sch.op("v", lambda e, G=GC: e.tensor_tensor(modv[:, G:G + 8], sc8[:, 16:24], sc8[:, 8:16], OP.add))
    sch.drain("v")
    mod_post = sch.op("v", lambda e: e.tensor_copy(sc8[:, 0:1], modv[:, 0:1]), inc=sV)
    if "mod" in tap_d:
        sch.wait("y", sV, mod_post)
        sch.op("y", lambda e: e.dma_start(out=tap_d["mod"], in_=modv[:]), inc=sO, n=16)
        tap_sync()

    if phase_limit <= 0:
        return _finish_early()
    cm_mrow.__exit__(None, None, None)
    cm_wmb.__exit__(None, None, None)
    wsl, cm_wsl = alloc("wsl", [128, 3 * 1024], BF16)
    xcbf, cm_xcbf = alloc("xcbf", [128, KT * L], BF16)
    E, cm_E = alloc("E", [128, 3 * L])
    hlnb, cm_hlnb = alloc("hlnb", [128, 2 * L], BF16)
    cvd, cm_cvd = alloc("cvd", [128, 2 * 5 * 128], BF16)
    onesc, cm_onesc = alloc("onesc", [128, 512], BF16)

    def emit_stats(src, eps, want_mean):
        RH = rwbf[0:1, 0 * SCW:1 * SCW]
        RL = rwbf[0:1, 1 * SCW:2 * SCW]
        MH = rwbf[0:1, 2 * SCW:3 * SCW]
        ML = rwbf[0:1, 3 * SCW:4 * SCW]
        eps_col = cst[0:1, 0:1] if eps == 1e-5 else cst[0:1, 1:2]
        mm_done = []
        nch = 0
        for c in range(NSC):
            for k in range(KT):
                par = nch % 2
                a_sl = chA[:, par * SCW:(par + 1) * SCW]
                b_sl = chA[:, (2 + par) * SCW:(3 + par) * SCW]
                if nch >= 2:
                    sch.wait("s", sT, mm_done[nch - 2])
                if want_mean:
                    sch.op("s", lambda e, a=a_sl, k=k, c=c: e.copy(a, src(k, c)))
                    sch.drain("s")
                s_sq = sch.op("s", lambda e, b=b_sl, k=k, c=c: e.activation(
                    b, src(k, c), AF.Square), inc=sS)
                sch.wait("t", sS, s_sq)
                if want_mean:
                    sch.op("t", lambda e, a=a_sl, c=c, k=k: e.matmul(
                        PS1[0:1, (c % 4) * 512:(c % 4) * 512 + SCW], ones_m[:], a,
                        start=(k == 0), stop=(k == KT - 1)))
                tm = sch.op("t", lambda e, b=b_sl, c=c, k=k: e.matmul(
                    PS0[0:1, (c % 4) * 512:(c % 4) * 512 + SCW], ones_m[:], b,
                    start=(k == 0), stop=(k == KT - 1)), inc=sT)
                mm_done.append(tm)
                nch += 1
        ret = None
        prev_spl = None
        prev_bc = None
        for c in range(NSC):
            msq_row = PS0[0:1, (c % 4) * 512:(c % 4) * 512 + SCW]
            mean_row = PS1[0:1, (c % 4) * 512:(c % 4) * 512 + SCW]
            sch.wait("s", sT, mm_done[c * KT + KT - 1])
            if prev_spl is not None:
                sch.wait("s", sV, prev_spl)
            sch.drain("s")
            if want_mean:
                s_m2 = sch.op("s", lambda e, m=mean_row: e.activation(rwA[:], m, AF.Square),
                              inc=sS)
                sch.wait("v", sS, s_m2)
                sch.drain("v")
                v_var = sch.op("v", lambda e, q=msq_row: e.tensor_tensor(
                    rwB[:], q, rwA[:], OP.subtract), inc=sV)
                sch.wait("s", sV, v_var)
            else:
                s_cp = sch.op("s", lambda e, q=msq_row: e.copy(rwB[:], q), inc=sS)
                sch.wait("s", sS, s_cp)
            sch.drain("s")
            sch.op("s", lambda e: e.activation(rwB[:], rwB[:], AF.Ln, bias=eps_col))
            sch.drain("s")
            s_rstd = sch.op("s", lambda e: e.activation(
                rwB[:], rwB[:], AF.Exp, scale=cst[0:1, 2:3]), inc=sS)
            sch.wait("v", sS, s_rstd)
            if prev_bc is not None:
                sch.wait("v", sT, prev_bc)
            sch.drain("v")
            sch.op("v", lambda e, RH=RH: e.tensor_copy(RH, rwB[:]))
            sch.drain("v")
            sch.op("v", lambda e, RH=RH: e.tensor_tensor(rwB[:], rwB[:], RH, OP.subtract))
            sch.drain("v")
            v_spl = sch.op("v", lambda e, RL=RL: e.tensor_copy(RL, rwB[:]), inc=sV)
            if want_mean:
                sch.op("v", lambda e, MH=MH, m=mean_row: e.tensor_copy(MH, m))
                sch.drain("v")
                sch.op("v", lambda e, MH=MH, m=mean_row: e.tensor_tensor(rwA[:], m, MH, OP.subtract))
                sch.drain("v")
                v_spl = sch.op("v", lambda e, ML=ML: e.tensor_copy(ML, rwA[:]), inc=sV)
            sch.wait("t", sV, v_spl)
            sch.op("t", lambda e, c=c, RH=RH: e.matmul(
                PS0[:, (c % 4) * 512:(c % 4) * 512 + SCW], ones_r[:], RH,
                start=True, stop=False))
            t_bc = sch.op("t", lambda e, c=c, RL=RL: e.matmul(
                PS0[:, (c % 4) * 512:(c % 4) * 512 + SCW], ones_r[:], RL,
                start=False, stop=True), inc=sT)
            if want_mean:
                sch.op("t", lambda e, c=c, MH=MH: e.matmul(
                    PS1[:, (c % 4) * 512:(c % 4) * 512 + SCW], ones_r[:], MH,
                    start=True, stop=False))
                t_bc = sch.op("t", lambda e, c=c, ML=ML: e.matmul(
                    PS1[:, (c % 4) * 512:(c % 4) * 512 + SCW], ones_r[:], ML,
                    start=False, stop=True), inc=sT)
            sch.wait("v", sT, t_bc)
            ret = sch.op("v", lambda e, c=c: e.tensor_copy(
                rstdB[:, c * SCW:(c + 1) * SCW],
                PS0[:, (c % 4) * 512:(c % 4) * 512 + SCW]), inc=sV)
            if want_mean:
                ret = sch.op("v", lambda e, c=c: e.tensor_copy(
                    meanB[:, c * SCW:(c + 1) * SCW],
                    PS1[:, (c % 4) * 512:(c % 4) * 512 + SCW]), inc=sV)
            prev_spl = ret
            prev_bc = t_bc
        return ret

    sch.wait("s", sX, x_loaded)
    sch.wait("t", sV, mod_done)
    sch.wait("v", sV, v_la8)
    ln1 = emit_stats(lambda k, c: x32[:, k * L + c * SCW: k * L + c * SCW + SCW], 1e-5, True)
    sch.wait("v", sV, ln1)
    sch.wait("s", sV, ln1)
    cv_ld = {}
    cv_used = {}

    def issue_cv(ct):
        if ct >= 2 and (ct - 2) in cv_used:
            sch.wait("g", sT, cv_used[ct - 2])
        sch.op("g", lambda e, ct=ct: e.dma_start(
            out=cvd[:, (ct % 2) * 640:(ct % 2) * 640 + 640],
            in_=convd_d[ct].rearrange("p a n -> p (a n)")), inc=sCV[ct % 2], n=16)
        cv_ld[ct] = (sCV[ct % 2], sCV[ct % 2].c)
    sch.wait("g", sT, mod_t_done)
    issue_cv(0)
    issue_cv(1)
    sch.drain("v")
    v_onesc = sch.op("v", lambda e: e.memset(onesc[:], 1.0), inc=sV)
    sch.drain("v")
    sch.wait("t", sV, v_onesc)
    xcb = []
    xc_mm = {}
    hln_s = {}
    for ct in range(KT):
        par = ct % 2
        if ct > 0:
            sch.wait("v", sS, hln_s[ct - 1])
        sch.drain("v")
        sch.op("v", lambda e, ct=ct: e.tensor_tensor(
            EP(0), x32[:, ct * L:(ct + 1) * L], meanB[:], OP.subtract))
        sch.drain("v")
        v1 = sch.op("v", lambda e, ct=ct: e.scalar_tensor_tensor(
            EP(0), EP(0), col(modv, M_CA1 + ct), rstdB[:], OP.mult, OP.mult), inc=sV)
        sch.wait("s", sV, v1)
        if ct >= 2:
            sch.wait("s", sT, xc_mm[ct - 2])
        sch.drain("s")
        s1 = sch.op("s", lambda e, ct=ct, par=par: e.activation(
            hlnb[:, par * L:(par + 1) * L], EP(0), AF.Identity,
            bias=col(modv, M_SA + ct)), inc=sS)
        hln_s[ct] = s1
        ps = PS0 if par == 0 else PS1
        sch.wait("t", sS, s1)
        sch.wait("t", *cv_ld[ct])
        if ct >= 2:
            sch.wait("t", sS, xcb[ct - 2])
        elif ct == 0:
            sch.wait("t", sV, ln1)
        t_cv = None
        for c in range(NSC):
            c0, c1 = c * 512, min((c + 1) * 512, L)
            t_cv = sch.op("t", lambda e, c0=c0, c1=c1, par=par, ps=ps: e.matmul(
                ps[:, c0:c1], cvd[:, par * 640 + 512: par * 640 + 640],
                onesc[:, 0:c1 - c0], start=True, stop=False, skip_group_check=True))
            for kk in range(4):
                sh = 3 - kk
                lo = max(sh, c0)
                if lo >= c1:
                    continue
                t_cv = sch.op("t", lambda e, kk=kk, lo=lo, c1=c1, sh=sh, par=par, ps=ps: e.matmul(
                    ps[:, lo:c1],
                    cvd[:, par * 640 + kk * 128: par * 640 + kk * 128 + 128],
                    hlnb[:, par * L + lo - sh: par * L + c1 - sh],
                    start=False, stop=(kk == 3), skip_group_check=True),
                    inc=sT if (kk == 3 and c1 == L) else None)
        xc_mm[ct] = t_cv
        cv_used[ct] = t_cv
        if ct + 2 < KT:
            issue_cv(ct + 2)
        sch.wait("s", sT, t_cv)
        sch.drain("s")
        s2 = sch.op("s", lambda e, ct=ct, ps=ps: e.copy(
            xcbf[:, ct * L:(ct + 1) * L], ps[:, 0:L]), inc=sS)
        xcb.append(s2)
        if "xc" in tap_d:
            sch.wait("y", sS, s2)
            sch.drain("s")
            s_tp = sch.op("s", lambda e, ct=ct: e.copy(EP(0), xcbf[:, ct * L:(ct + 1) * L]), inc=sS)
            sch.wait("y", sS, s_tp)
            sch.op("y", lambda e, ct=ct: e.dma_start(out=tap_d["xc"][ct], in_=EP(0)), inc=sO, n=16)
            tap_sync()
    xcbf_all = xcb[-1]

    wr_ld, wi_ld, ow_ld = {}, {}, {}

    def slice_dma(slot, src, ld_map, key):
        sch.op("g", lambda e, d=slot, s=src: e.dma_start(
            out=wsl[:, d * 1024:(d + 1) * 1024], in_=s.rearrange("p a b -> p (a b)")),
            inc=sWS[slot], n=16)
        ld_map[key] = (sWS[slot], sWS[slot].c)

    sch.wait("g", sT, mod_t_done)
    slice_dma(0, wr_d[0], wr_ld, 0)
    slice_dma(2, wi_d[0], wi_ld, 0)
    slice_dma(1, wr_d[1], wr_ld, 1)

    r_free = i_free = None
    vh = None
    vh_prev = None
    sch.wait("v", sS, xcbf_all)
    for ct in range(KT):
        sch.wait("t", sS, xcbf_all)
        sch.wait("t", *wr_ld[ct])
        if r_free:
            sch.wait("t", sS, r_free[0])
            sch.wait("t", sV, r_free[1])
        for c in range(NSC):
            for k in range(KT):
                t_r = sch.op("t", lambda e, ct=ct, c=c, k=k: e.matmul(
                    PS0[:, c * 512: c * 512 + SCW],
                    wsl[:, (ct % 2) * 1024 + k * 128:(ct % 2) * 1024 + k * 128 + 128],
                    xcbf[:, k * L + c * SCW: k * L + c * SCW + SCW],
                    start=(k == 0), stop=(k == KT - 1)),
                    inc=sT if (c == NSC - 1 and k == KT - 1) else None)
        sch.wait("t", *wi_ld[ct])
        if i_free:
            sch.wait("t", sV, i_free)
        for c in range(NSC):
            for k in range(KT):
                t_i = sch.op("t", lambda e, ct=ct, c=c, k=k: e.matmul(
                    PS1[:, c * 512: c * 512 + SCW],
                    wsl[:, 2 * 1024 + k * 128: 2 * 1024 + k * 128 + 128],
                    xcbf[:, k * L + c * SCW: k * L + c * SCW + SCW],
                    start=(k == 0), stop=(k == KT - 1)),
                    inc=sT if (c == NSC - 1 and k == KT - 1) else None)
        if ct + 2 <= KT - 1:
            sch.wait("g", sT, t_r)
            slice_dma(ct % 2, wr_d[ct + 2], wr_ld, ct + 2)
        if ct + 1 <= KT - 1:
            sch.wait("g", sT, t_i)
            slice_dma(2, wi_d[ct + 1], wi_ld, ct + 1)
        sch.wait("s", sT, t_r)
        sch.drain("s")
        s_rs = sch.op("s", lambda e, ct=ct: e.activation(
            PS0[:, 0:L], PS0[:, 0:L], AF.Sigmoid, bias=col(vecs, V_WRB + ct)), inc=sS)
        sch.wait("s", sT, t_i)
        sch.drain("s")
        s_is = sch.op("s", lambda e, ct=ct: e.activation(
            PS1[:, 0:L], PS1[:, 0:L], AF.Sigmoid, bias=col(vecs, V_WIB + ct)), inc=sS)
        if "r" in tap_d:
            if vh_prev:
                sch.wait("s", sV, vh_prev)
            s_tp = sch.op("s", lambda e: e.copy(EP(0), PS0[:, 0:L]), inc=sS)
            sch.wait("y", sS, s_tp)
            sch.op("y", lambda e, ct=ct: e.dma_start(out=tap_d["r"][ct], in_=EP(0)), inc=sO, n=16)
            tap_sync()
        if vh_prev:
            sch.wait("s", sV, vh_prev)
        sch.drain("s")
        s_a = sch.op("s", lambda e, ct=ct: e.activation(
            EP(1), PS0[:, 0:L], AF.Exp, scale=col(modv, M_LA8 + ct)), inc=sS)
        sch.wait("v", sS, s_rs)
        sch.drain("v")
        v_la = sch.op("v", lambda e, ct=ct: e.tensor_scalar(
            EP(0), PS0[:, 0:L], col(modv, M_LA8 + ct), LOG1EM7, OP.mult, OP.max), inc=sV)
        r_free = (s_a, v_la)
        sch.drain("v")
        sch.op("v", lambda e: e.tensor_tensor_scan(EP(0), EP(0), EP(0), 0.0, OP.add, OP.bypass))
        sch.drain("v")
        v_cl = sch.op("v", lambda e: e.tensor_scalar(EP(0), EP(0), -80.0, None, OP.max), inc=sV)
        sch.drain("s")
        sch.op("s", lambda e: e.activation(EP(1), EP(1), AF.Square))
        sch.drain("s")
        sch.op("s", lambda e: e.activation(EP(1), EP(1), AF.Relu,
                                           scale=cst[:, 3:4], bias=cst[:, 5:6]))
        sch.drain("s")
        s_b0 = sch.op("s", lambda e: e.activation(
            EP(1), EP(1), AF.Sqrt, bias=cst[:, 6:7]), inc=sS)
        sch.wait("v", sS, s_b0)
        sch.drain("v")
        v_bi = sch.op("v", lambda e: e.tensor_tensor(EP(2), EP(1), PS1[:, 0:L], OP.mult), inc=sV)
        i_free = v_bi
        sch.drain("v")
        v_bxc = sch.op("v", lambda e, ct=ct: e.tensor_tensor(
            EP(2), EP(2), xcbf[:, ct * L:(ct + 1) * L], OP.mult), inc=sV)
        sch.wait("s", sV, v_bxc)
        sch.drain("s")
        s_A = sch.op("s", lambda e: e.activation(EP(1), EP(0), AF.Exp), inc=sS)
        sch.drain("s")
        s_u = sch.op("s", lambda e: e.activation(EP(0), EP(0), AF.Exp, scale=cst[:, 3:4]), inc=sS)
        sch.wait("v", sS, s_u)
        sch.drain("v")
        sch.op("v", lambda e: e.tensor_tensor(EP(2), EP(2), EP(0), OP.mult))
        sch.drain("v")
        sch.op("v", lambda e: e.tensor_tensor_scan(EP(2), EP(2), EP(2), 0.0, OP.add, OP.bypass))
        sch.drain("v")
        vh = sch.op("v", lambda e, ct=ct: e.tensor_tensor(
            hbf[:, ct * L:(ct + 1) * L], EP(1), EP(2), OP.mult), inc=sV)
        vh_prev = vh
        if "h" in tap_d:
            sch.wait("s", sV, vh)
            sch.drain("s")
            s_tap = sch.op("s", lambda e, ct=ct: e.copy(EP(2), hbf[:, ct * L:(ct + 1) * L]), inc=sS)
            sch.wait("y", sS, s_tap)
            sch.op("y", lambda e, ct=ct: e.dma_start(out=tap_d["h"][ct], in_=EP(2)), inc=sO, n=16)
            tap_sync()
    h_all = vh

    if phase_limit <= 2:
        return _finish_early()
    cm_onesc.__exit__(None, None, None)
    cm_cvd.__exit__(None, None, None)
    cm_hlnb.__exit__(None, None, None)
    cm_E.__exit__(None, None, None)
    cm_xcbf.__exit__(None, None, None)
    y32, cm_y32 = alloc("y32", [128, KT * L])

    sch.wait("g", sT, sT.c)
    slice_dma(0, ow_d[0], ow_ld, 0)
    slice_dma(1, ow_d[1], ow_ld, 1)
    y_ev = {}
    sch.wait("t", sV, h_all)
    sch.wait("t", sS, r_free[0])
    sch.wait("t", sV, r_free[1])
    sch.wait("t", sV, i_free)
    for do in range(KT):
        sch.wait("t", *ow_ld[do])
        if do >= 2:
            sch.wait("t", sV, y_ev[do - 2])
        ps = PS0 if do % 2 == 0 else PS1
        t_y = None
        for c in range(NSC):
            for k in range(KT):
                t_y = sch.op("t", lambda e, do=do, c=c, k=k, ps=ps: e.matmul(
                    ps[:, c * 512: c * 512 + SCW],
                    wsl[:, (do % 2) * 1024 + k * 128:(do % 2) * 1024 + k * 128 + 128],
                    hbf[:, k * L + c * SCW: k * L + c * SCW + SCW],
                    start=(k == 0), stop=(k == KT - 1)),
                    inc=sT if (c == NSC - 1 and k == KT - 1) else None)
        if do + 2 < KT:
            sch.wait("g", sT, t_y)
            slice_dma(do % 2, ow_d[do + 2], ow_ld, do + 2)
        sch.wait("v", sT, t_y)
        y_ev[do] = sch.op("v", lambda e, do=do, ps=ps: e.tensor_copy(
            y32[:, do * L:(do + 1) * L], ps[:, 0:L]), inc=sV)
    if "y" in tap_d:
        sch.wait("y", sV, y_ev[KT - 1])
        for k in range(KT):
            sch.op("y", lambda e, k=k: e.dma_start(
                out=tap_d["y"][k], in_=y32[:, k * L:(k + 1) * L]), inc=sO, n=16)
        tap_sync()
    sch.wait("s", sV, y_ev[KT - 1])
    sch.wait("t", sV, y_ev[KT - 1])
    rms = emit_stats(lambda k, c: y32[:, k * L + c * SCW: k * L + c * SCW + SCW], 1e-6, False)
    sch.wait("v", sV, rms)
    if "rstd" in tap_d:
        sch.wait("y", sV, rms)
        sch.op("y", lambda e: e.dma_start(out=tap_d["rstd"], in_=rstdB[:]), inc=sO, n=16)
        tap_sync()
    vr = None
    for ct in range(KT):
        sch.drain("v")
        sch.op("v", lambda e, ct=ct: e.scalar_tensor_tensor(
            y32[:, ct * L:(ct + 1) * L], y32[:, ct * L:(ct + 1) * L],
            col(vecs, V_NORMW + ct), rstdB[:], OP.mult, OP.mult))
        sch.drain("v")
        vr = sch.op("v", lambda e, ct=ct: e.scalar_tensor_tensor(
            x32[:, ct * L:(ct + 1) * L], y32[:, ct * L:(ct + 1) * L], col(modv, M_GAC + ct),
            x32[:, ct * L:(ct + 1) * L], OP.mult, OP.add), inc=sV)
    xattn = vr
    if "xattn" in tap_d:
        sch.wait("y", sV, xattn)
        for k in range(KT):
            sch.op("y", lambda e, k=k: e.dma_start(
                out=tap_d["xattn"][k], in_=x32[:, k * L:(k + 1) * L]), inc=sO, n=16)
        tap_sync()

    if phase_limit <= 3:
        return _finish_early()
    cm_y32.__exit__(None, None, None)
    cm_wsl.__exit__(None, None, None)
    scr4, cm_scr4 = alloc("scr4", [128, L])
    eye_s, cm_eye = alloc("eye_s", [128, 128])
    sch.wait("y", sV, xattn)
    sch.op("y", lambda e: e.dma_start(out=eye_s[:], in_=eye_d), inc=sEY, n=16)
    eye_ld = sEY.c
    scoS, cm_scoS = alloc("scoS", [128, TT * 8])
    wtok, cm_wtok = alloc("wtok", [128, TT * 8])
    rts, cm_rts = alloc("rts", [128, TT * 6])

    sch.wait("s", sV, xattn)
    sch.wait("t", sV, rms)
    ln2 = emit_stats(lambda k, c: x32[:, k * L + c * SCW: k * L + c * SCW + SCW], 1e-5, True)
    sch.wait("v", sV, ln2)
    hf = None
    hf_ct = []
    for ct in range(KT):
        sch.op("v", lambda e, ct=ct: e.tensor_tensor(
            scr4[:], x32[:, ct * L:(ct + 1) * L], meanB[:], OP.subtract))
        sch.drain("v")
        vln = sch.op("v", lambda e, ct=ct: e.scalar_tensor_tensor(
            scr4[:], scr4[:], col(modv, M_CB1 + ct), rstdB[:], OP.mult, OP.mult), inc=sV)
        sch.wait("s", sV, vln)
        sch.drain("s")
        hf = sch.op("s", lambda e, ct=ct: e.activation(
            hffn[:, ct * L:(ct + 1) * L], scr4[:], AF.Identity,
            bias=col(modv, M_SB + ct)), inc=sS)
        hf_ct.append(hf)
        sch.wait("v", sS, hf)
    hffn_all = hf
    HFFN_SEM = "s"
    if "hffn" in tap_d:
        for k in range(KT):
            sch.wait("s", sS, hffn_all) if False else None
            sch.drain("s")
            s_tap = sch.op("s", lambda e, k=k: e.copy(scr4[:], hffn[:, k * L:(k + 1) * L]), inc=sS)
            sch.wait("y", sS, s_tap)
            sch.op("y", lambda e, k=k: e.dma_start(out=tap_d["hffn"][k], in_=scr4[:]), inc=sO, n=16)
            tap_sync()

    sch.wait("t", sGW, gw_ld)
    sch.wait("t", sV, ln2)
    t_sc = None
    for k in range(KT):
        sch.wait("t", sS, hf_ct[k])
        for tt in range(TT):
            t_sc = sch.op("t", lambda e, tt=tt, k=k: e.matmul(
                PS0[:, tt * 8:(tt + 1) * 8],
                hffn[:, k * L + tt * 128: k * L + tt * 128 + 128],
                gw_s[:, k * 8:(k + 1) * 8],
                start=(k == 0 and tt == 0), stop=(k == KT - 1),
                skip_group_check=True),
                inc=sT if (k == KT - 1 and tt == TT - 1) else None)
    sch.wait("v", sT, t_sc)
    sch.drain("v")
    sco_all = sch.op("v", lambda e: e.tensor_copy(scoS[:], PS0[:, 0:TT * 8]), inc=sV)
    sch.wait("v", sV, sco_all)
    R_M, R_SUM, R_M1, R_M2, R_DEN = 0, 1, 2, 3, 4

    def rcol(q, tt):
        return rts[:, q * TT + tt: q * TT + tt + 1]

    def sco(tt):
        return scoS[:, tt * 8:(tt + 1) * 8]

    def wtk(tt):
        return wtok[:, tt * 8:(tt + 1) * 8]

    sch.drain("v")
    for tt in range(TT):
        sch.op("v", lambda e, tt=tt: e.reduce_max(rcol(R_M, tt), sco(tt), axis=AX.X))
    sch.drain("v")
    v_sh = None
    for tt in range(TT):
        v_sh = sch.op("v", lambda e, tt=tt: e.tensor_scalar(
            wtk(tt), sco(tt), rcol(R_M, tt), None, OP.subtract),
            inc=sV if tt == TT - 1 else None)
    sch.wait("s", sV, v_sh)
    sch.drain("s")
    s_pe = None
    for tt in range(TT):
        s_pe = sch.op("s", lambda e, tt=tt: e.activation(sco(tt), wtk(tt), AF.Exp),
                      inc=sS if tt == TT - 1 else None)
    sch.wait("v", sS, s_pe)
    for tt in range(TT):
        sch.op("v", lambda e, tt=tt: e.tensor_reduce(rcol(R_SUM, tt), sco(tt), axis=AX.X, op=OP.add))
    sch.drain("v")
    for tt in range(TT):
        sch.op("v", lambda e, tt=tt: e.reciprocal(rcol(R_DEN, tt), rcol(R_SUM, tt)))
    sch.drain("v")
    for tt in range(TT):
        sch.op("v", lambda e, tt=tt: e.tensor_scalar(sco(tt), sco(tt), rcol(R_DEN, tt), None, OP.mult))
    sch.drain("v")
    if "probs" in tap_d:
        v_pr = sch.op("v", lambda e: e.tensor_copy(sc8[:, 2:3], scoS[:, 0:1]), inc=sV)
        sch.wait("y", sV, v_pr)
        sch.op("y", lambda e: e.dma_start(out=tap_d["probs"], in_=scoS[:]), inc=sO, n=16)
        tap_sync()
    for tt in range(TT):
        sch.op("v", lambda e, tt=tt: e.reduce_max(rcol(R_M1, tt), sco(tt), axis=AX.X))
    sch.drain("v")
    for tt in range(TT):
        sch.op("v", lambda e, tt=tt: e.tensor_scalar(wtk(tt), sco(tt), rcol(R_M1, tt), None, OP.is_ge))
    sch.drain("v")
    for tt in range(TT):
        sch.op("v", lambda e, tt=tt: e.scalar_tensor_tensor(wtk(tt), wtk(tt), -2.0, sco(tt), OP.mult, OP.add))
    sch.drain("v")
    for tt in range(TT):
        sch.op("v", lambda e, tt=tt: e.reduce_max(rcol(R_M2, tt), wtk(tt), axis=AX.X))
    sch.drain("v")
    for tt in range(TT):
        sch.op("v", lambda e, tt=tt: e.tensor_scalar(wtk(tt), sco(tt), rcol(R_M2, tt), None, OP.is_ge))
    for tt in range(TT):
        sch.op("v", lambda e, tt=tt: e.tensor_tensor(rcol(R_DEN, tt), rcol(R_M1, tt), rcol(R_M2, tt), OP.add))
    sch.drain("v")
    for tt in range(TT):
        sch.op("v", lambda e, tt=tt: e.reciprocal(rcol(R_M, tt), rcol(R_DEN, tt)))
    sch.drain("v")
    for tt in range(TT):
        sch.op("v", lambda e, tt=tt: e.tensor_tensor(wtk(tt), wtk(tt), sco(tt), OP.mult))
    sch.drain("v")
    v_w = None
    for tt in range(TT):
        v_w = sch.op("v", lambda e, tt=tt: e.tensor_scalar(
            wtk(tt), wtk(tt), rcol(R_M, tt), None, OP.mult),
            inc=sV if tt == TT - 1 else None)
    if "wsel" in tap_d:
        sch.wait("y", sV, v_w)
        sch.op("y", lambda e: e.dma_start(out=tap_d["wsel"], in_=wtok[:]), inc=sO, n=16)
        tap_sync()
    sch.wait("t", sV, v_w)
    sch.wait("t", sEY, eye_ld)
    wt_ev = {}
    for tt in range(TT):
        if tt >= 4:
            sch.wait("t", sV, wt_ev[tt - 4])
        t_tr = sch.op("t", lambda e, tt=tt: e.transpose(
            PS1[0:8, (tt % 4) * 512:(tt % 4) * 512 + 128], wtk(tt), eye_s[:]), inc=sT)
        sch.wait("v", sT, t_tr)
        wt_ev[tt] = sch.op("v", lambda e, tt=tt: e.tensor_copy(
            wT_s[0:8, tt * 128:(tt + 1) * 128],
            PS1[0:8, (tt % 4) * 512:(tt % 4) * 512 + 128]), inc=sV)
    v_wt = wt_ev[TT - 1]

    if phase_limit <= 4:
        return _finish_early()
    for cm in (cm_rts, cm_wtok, cm_scoS, cm_eye, cm_scr4,
               cm_chA, cm_rwbf, cm_rwB, cm_rwA, cm_rstdB, cm_meanB):
        cm.__exit__(None, None, None)
    sel_s, _p5s = alloc("sel_s", [8, 8 * 128])
    siluT, _p5t = alloc("siluT", [128, 2 * 512], BF16)
    gS, cm_gS = alloc("gS", [128, FT * TCW], BF16)
    rbB, cm_rbB = alloc("rbB", [128, 2 * TCW])
    hrb, cm_hrb = alloc("hrb", [128, 2 * KT * TCW], BF16)
    w13b, cm_w13 = alloc("w13b", [128, 2 * 2048], BF16)
    w2b, cm_w2 = alloc("w2b", [128, 2 * 2048], BF16)

    w13_ld, w2_ld = {}, {}
    g_ev = {}
    out_ev = {}
    t_g_done = {}
    t_out_done = {}
    sch.wait("g", sV, v_wt)
    sch.op("g", lambda e: e.dma_start(out=sel_s[:], in_=sel8_d.rearrange("k e m -> k (e m)")),
           inc=sGW, n=16)
    sel_ld = sGW.c

    hrb_ready = {}

    def emit_rb_hrb(tc, ei, wait_g_free=None, wait_ps_free=None):
        par = ei % 2
        sch.wait("t", sV, v_wt)
        sch.wait("t", sGW, sel_ld)
        if wait_ps_free is not None:
            sch.wait("t", sV, wait_ps_free)
        t_rb = None
        for c2 in range(NCC2):
            t_rb = sch.op("t", lambda e, ei=ei, tc=tc, c2=c2: e.matmul(
                PS0[:, c2 * 512: c2 * 512 + CCW],
                sel_s[0:8, ei * 128:(ei + 1) * 128],
                wT_s[0:8, tc * TCW + c2 * CCW: tc * TCW + c2 * CCW + CCW],
                start=True, stop=True), inc=sT)
        sch.wait("v", sT, t_rb)
        sch.drain("v")
        sch.op("v", lambda e, par=par: e.tensor_copy(
            rbB[:, par * TCW:(par + 1) * TCW], PS0[:, 0:TCW]))
        sch.drain("v")
        ev = None
        for k in range(KT):
            ev = sch.op("v", lambda e, k=k, tc=tc, par=par: e.tensor_tensor(
                hrb[:, par * KT * TCW + k * TCW: par * KT * TCW + (k + 1) * TCW],
                hffn[:, k * L + tc * TCW: k * L + tc * TCW + TCW],
                rbB[:, par * TCW:(par + 1) * TCW], OP.mult), inc=sV)
        hrb_ready[(tc, ei)] = ev
        return t_rb

    for tc in range(NTC):
        for ei in range(NE):
            par = ei % 2
            hbase = par * KT * TCW

            def pf_w13(f, ei=ei, tc=tc):
                prev = (tc, ei, f - 2)
                if f >= 2 and prev in t_g_done:
                    sch.wait("g", sT, t_g_done[prev])
                sch.op("g", lambda e, ei=ei, f=f: e.dma_start(
                    out=w13b[:, (f % 2) * 2048:(f % 2) * 2048 + 2048],
                    in_=w13_d[ei, f].rearrange("p a b c -> p (a b c)")), inc=sW13[f % 2], n=16)
                w13_ld[(tc, ei, f)] = (sW13[f % 2], sW13[f % 2].c)

            def pf_w2(do, ei=ei, tc=tc):
                prev = (tc, ei, do - 2)
                if do >= 2 and prev in t_out_done:
                    sch.wait("g", sT, t_out_done[prev])
                sch.op("g", lambda e, ei=ei, do=do: e.dma_start(
                    out=w2b[:, (do % 2) * 2048:(do % 2) * 2048 + 2048],
                    in_=w2t_d[ei, do].rearrange("p a b -> p (a b)")), inc=sW2[do % 2], n=16)
                w2_ld[(tc, ei, do)] = (sW2[do % 2], sW2[do % 2].c)

            pe = (tc, ei - 1) if ei > 0 else ((tc - 1, NE - 1) if tc > 0 else None)
            if pe is not None:
                sch.wait("g", sT, t_g_done[(pe[0], pe[1], FT - 1)])
            pf_w13(0)
            pf_w13(1)

            if (tc, ei) not in hrb_ready:
                wait_ps = None
                if pe is not None:
                    wait_ps = g_ev[(pe[0], pe[1], FT - 1, NCC2 - 1)]
                emit_rb_hrb(tc, ei, wait_ps_free=wait_ps)

            sch.wait("t", sV, hrb_ready[(tc, ei)])
            for f in range(FT):
                sch.wait("t", *w13_ld[(tc, ei, f)])
                tg = None
                for c2 in range(NCC2):
                    kp = (tc, ei, f - 1, c2) if f > 0 else \
                        ((tc, ei - 1, FT - 1, c2) if ei > 0 else
                         ((tc - 1, NE - 1, FT - 1, c2) if tc > 0 else None))
                    if kp is not None and kp in g_ev:
                        sch.wait("t", sV, g_ev[kp])
                    for k in range(KT):
                        sch.op("t", lambda e, f=f, c2=c2, k=k, tc=tc: e.matmul(
                            PS0[:, c2 * 512: c2 * 512 + CCW],
                            w13b[:, (f % 2) * 2048 + k * 128:(f % 2) * 2048 + k * 128 + 128],
                            hffn[:, k * L + tc * TCW + c2 * CCW: k * L + tc * TCW + c2 * CCW + CCW],
                            start=(k == 0), stop=(k == KT - 1)))
                    tg = None
                    for k in range(KT):
                        tg = sch.op("t", lambda e, f=f, c2=c2, k=k, hbase=hbase: e.matmul(
                            PS0[:, (NCC2 + c2) * 512:(NCC2 + c2) * 512 + CCW],
                            w13b[:, (f % 2) * 2048 + 1024 + k * 128:(f % 2) * 2048 + 1024 + k * 128 + 128],
                            hrb[:, hbase + k * TCW + c2 * CCW: hbase + k * TCW + c2 * CCW + CCW],
                            start=(k == 0), stop=(k == KT - 1)),
                            inc=sT if k == KT - 1 else None)
                    sch.wait("s", sT, tg)
                    spar = (f * NCC2 + c2) % 2
                    if g_ev:
                        sch.wait("s", sV, max(v for v in g_ev.values()))
                    sch.drain("s")
                    s_si = sch.op("s", lambda e, c2=c2, spar=spar: e.activation(
                        siluT[:, spar * 512: spar * 512 + CCW],
                        PS0[:, c2 * 512: c2 * 512 + CCW], AF.Silu), inc=sS)
                    sch.wait("v", sS, s_si)
                    g_ev[(tc, ei, f, c2)] = sch.op("v", lambda e, f=f, c2=c2, tc=tc, spar=spar: e.tensor_tensor(
                        gS[:, f * TCW + c2 * CCW: f * TCW + c2 * CCW + CCW],
                        siluT[:, spar * 512: spar * 512 + CCW],
                        PS0[:, (NCC2 + c2) * 512:(NCC2 + c2) * 512 + CCW], OP.mult), inc=sV)
                t_g_done[(tc, ei, f)] = tg
                if f + 2 < FT:
                    pf_w13(f + 2)
            nxt = (tc, ei + 1) if ei + 1 < NE else None
            if nxt is not None:
                emit_rb_hrb(nxt[0], nxt[1],
                            wait_ps_free=g_ev[(tc, ei, FT - 1, NCC2 - 1)])
            pf_w2(0)
            pf_w2(1)
            for do in range(KT):
                sch.wait("t", *w2_ld[(tc, ei, do)])
                keyp = (tc, ei, do - 2) if do >= 2 else \
                    ((tc, ei - 1, KT - 2 + do) if ei > 0 else
                     ((tc - 1, NE - 1, KT - 2 + do) if tc > 0 else None))
                if keyp is not None and keyp in out_ev:
                    sch.wait("t", sV, out_ev[keyp])
                sch.wait("t", sV, g_ev[(tc, ei, FT - 1, NCC2 - 1)])
                t_o = None
                for c2 in range(NCC2):
                    for fk in range(FT):
                        t_o = sch.op("t", lambda e, do=do, c2=c2, fk=fk: e.matmul(
                            PS1[:, (do % 2) * 1024 + c2 * 512:(do % 2) * 1024 + c2 * 512 + CCW],
                            w2b[:, (do % 2) * 2048 + fk * 128:(do % 2) * 2048 + fk * 128 + 128],
                            gS[:, fk * TCW + c2 * CCW: fk * TCW + c2 * CCW + CCW],
                            start=(fk == 0), stop=(fk == FT - 1)),
                            inc=sT if (c2 == NCC2 - 1 and fk == FT - 1) else None)
                t_out_done[(tc, ei, do)] = t_o
                if do + 2 < KT:
                    pf_w2(do + 2)
                sch.wait("v", sT, t_o)
                sch.drain("v")
                out_ev[(tc, ei, do)] = sch.op("v", lambda e, do=do, tc=tc: e.scalar_tensor_tensor(
                    x32[:, do * L + tc * TCW: do * L + tc * TCW + TCW],
                    PS1[:, (do % 2) * 1024:(do % 2) * 1024 + TCW],
                    col(modv, M_GBC + do),
                    x32[:, do * L + tc * TCW: do * L + tc * TCW + TCW],
                    OP.mult, OP.add), inc=sV)
        if phase_limit <= 7:
            return _finish_early()
        for ct in range(KT):
            sch.wait("y", sV, out_ev[(tc, NE - 1, ct)])
            sch.op("y", lambda e, ct=ct, tc=tc: e.dma_start(
                out=y_d[ct][:, tc * TCW:(tc + 1) * TCW],
                in_=x32[:, ct * L + tc * TCW: ct * L + tc * TCW + TCW]), inc=sO, n=16)

    sch.wait("y", sO, sO.c)

    return _replay()


_NC_CACHE = {}


def kernel(**inputs):
    n_cores = 8
    in_maps = host_prep(inputs, n_cores=n_cores, L=2048)
    if "nc" not in _NC_CACHE:
        _NC_CACHE["nc"] = build(L=2048)
    from concourse.bass_utils import run_bass_kernel_spmd
    res = run_bass_kernel_spmd(_NC_CACHE["nc"], in_maps, core_ids=list(range(n_cores)))
    return host_post(res.results)


# version 06
# speedup vs baseline: 1.0307x; 1.0307x over previous
import numpy as np
import concourse.bass as bass
import concourse.mybir as mybir

F32 = mybir.dt.float32
BF16 = mybir.dt.bfloat16
AF = mybir.ActivationFunctionType
OP = mybir.AluOpType
AX = mybir.AxisListType

D = 1024
KT = 8
FT = 16
NE = 8
NMOD = 48
LOG1EM7 = float(np.float32(np.log(np.float32(1e-7))))

ENGS = ("t", "v", "s", "g", "y")


class Sem:
    def __init__(self, h):
        self.h = h
        self.c = 0


class Sched:
    def __init__(self):
        self.q = {e: [] for e in ENGS}

    def op(self, eng, build, inc=None, n=1):
        if inc is not None:
            inc.c += n
            h = inc.h

            def f(e, build=build, h=h, n=n):
                build(e).then_inc(h, n)
            self.q[eng].append(f)
            return inc.c
        self.q[eng].append(lambda e, build=build: build(e))
        return None

    def wait(self, eng, sem, val):
        assert val is not None and val <= sem.c, (eng, val, sem.c)
        if val > 0:
            self.q[eng].append(lambda e, h=sem.h, v=val: e.wait_ge(h, v))

    def drain(self, eng):
        self.q[eng].append(lambda e: e.drain())


def host_prep(inputs, n_cores=8, L=2048):
    A = np.ascontiguousarray
    x = np.asarray(inputs["L0_attn_x"], dtype=np.float32)
    conv_w = np.asarray(inputs["L0_attn_conv_weight"], np.float32)
    conv_b = np.asarray(inputs["L0_attn_conv_bias"], np.float32)
    wr = np.asarray(inputs["L0_attn_Wr_w"], np.float32)
    wrb = np.asarray(inputs["L0_attn_Wr_b"], np.float32)
    wi = np.asarray(inputs["L0_attn_Wi_w"], np.float32)
    wib = np.asarray(inputs["L0_attn_Wi_b"], np.float32)
    log_a = np.asarray(inputs["L0_attn_log_a"], np.float32)
    ow = np.asarray(inputs["L0_attn_out_w"], np.float32)
    nw = np.asarray(inputs["L0_attn_norm_w"], np.float32)
    gw = np.asarray(inputs["L0_ffn_gate_w"], np.float32)
    w1 = np.asarray(inputs["L0_ffn_w1"], np.float32)
    w2 = np.asarray(inputs["L0_ffn_w2"], np.float32)
    w3 = np.asarray(inputs["L0_ffn_w3"], np.float32)
    c = np.asarray(inputs["L0_c"], np.float32)
    wmod = np.asarray(inputs["L0_wmod"], np.float32)

    def vec8(v):
        return A(v.reshape(8, 128).T)

    def sq_t(w):
        return A(w.T.reshape(8, 128, 8, 128).transpose(2, 1, 0, 3))

    def ffn_t(w):
        return w.transpose(0, 2, 1).reshape(8, 8, 128, 16, 128).transpose(0, 3, 2, 1, 4)

    w13 = A(np.stack([ffn_t(w1), ffn_t(w3)], axis=3))
    w2t = A(w2.transpose(0, 2, 1).reshape(8, 16, 128, 8, 128).transpose(0, 3, 2, 1, 4))
    wmod_t = A(wmod.reshape(8, 128, 12, 512))
    gwt = A(gw.T.reshape(8, 128, 8))

    convd = np.zeros((8, 5, 128, 128), np.float32)
    ii = np.arange(128)
    for ct in range(8):
        for k in range(4):
            convd[ct, k, ii, ii] = conv_w[ct * 128:(ct + 1) * 128, 0, k]
        convd[ct, 4, ii, ii] = conv_b[ct * 128:(ct + 1) * 128]
    shared = {
        "convd": A(convd.transpose(0, 2, 1, 3)),
        "convw": A(conv_w[:, 0, :].reshape(8, 128, 4).transpose(1, 0, 2).reshape(128, 32)),
        "convb": vec8(conv_b), "wrb": vec8(wrb), "wib": vec8(wib),
        "loga": vec8(log_a), "normw": vec8(nw),
        "wr": sq_t(wr), "wi": sq_t(wi), "ow": sq_t(ow),
        "gwt": gwt, "w13": w13, "w2t": w2t, "wmod": wmod_t,
        "eye": np.eye(128, dtype=np.float32),
        "sel8": np.ascontiguousarray(np.broadcast_to(
            np.eye(8, dtype=np.float32)[:, :, None], (8, 8, 128))),
    }
    xt = x.transpose(0, 2, 1)
    in_maps = []
    for b in range(n_cores):
        m = dict(shared)
        m["x"] = A(xt[b].reshape(8, 128, L))
        m["cvec"] = vec8(c[b])
        in_maps.append(m)
    return in_maps


def host_post(results):
    outs = []
    for r in results:
        y = np.asarray(r["y"])
        outs.append(np.ascontiguousarray(y.reshape(1024, -1).T))
    return np.stack(outs, axis=0)


def build(L=2048, taps=(), phase_limit=99):
    taps = set(taps)
    nc = bass.Bass("TRN2", target_bir_lowering=False)

    SCW = min(512, L)
    NSC = L // SCW
    TCW = min(1024, L)
    NTC = L // TCW
    CCW = min(512, TCW)
    NCC2 = TCW // CCW
    TT = L // 128

    dr = lambda name, shape: nc.dram_tensor(name, shape, F32, kind="ExternalInput").ap()
    x_d = dr("x", [KT, 128, L])
    cvec_d = dr("cvec", [128, 8])
    convw_d = dr("convw", [128, 32])
    convd_d = dr("convd", [8, 128, 5, 128])
    convb_d = dr("convb", [128, 8])
    wrb_d = dr("wrb", [128, 8])
    wib_d = dr("wib", [128, 8])
    loga_d = dr("loga", [128, 8])
    normw_d = dr("normw", [128, 8])
    wr_d = dr("wr", [8, 128, 8, 128])
    wi_d = dr("wi", [8, 128, 8, 128])
    ow_d = dr("ow", [8, 128, 8, 128])
    gwt_d = dr("gwt", [8, 128, 8])
    w13_d = dr("w13", [NE, FT, 128, 2, 8, 128])
    w2t_d = dr("w2t", [NE, 8, 128, 16, 128])
    wmod_d = dr("wmod", [8, 128, 12, 512])
    eye_d = dr("eye", [128, 128])
    sel8_d = dr("sel8", [8, 8, 128])
    y_d = nc.dram_tensor("y", [KT, 128, L], F32, kind="ExternalOutput").ap()
    tap_d = {}
    for t in taps:
        shp = {"probs": [128, TT * 8], "wsel": [128, TT * 8],
               "mod": [128, 56], "rstd": [128, L], "mean": [128, L]}.get(t, [KT, 128, L])
        tap_d[t] = nc.dram_tensor("tap_" + t, shp, F32, kind="ExternalOutput").ap()

    sch = Sched()

    def alloc(name, shape, dt=F32):
        cm = nc.sbuf_tensor(name, shape, dt)
        return cm.__enter__(), cm

    def palloc(name, shape):
        cm = nc.psum_tensor(name, shape, F32)
        return cm.__enter__(), cm

    sem_cms = []

    def mksem(name):
        cm = nc.semaphore(name)
        sem_cms.append(cm)
        return Sem(cm.__enter__())

    sX = mksem("sX")
    sXV = mksem("sXV")
    sGW = mksem("sGW")
    sEY = mksem("sEY")
    sWM = [mksem(f"sWM{i}") for i in range(4)]
    sWS = [mksem(f"sWS{i}") for i in range(3)]
    sCV = [mksem("sCV0"), mksem("sCV1")]
    sW13 = [mksem("sW13a"), mksem("sW13b")]
    sW2 = [mksem("sW2a"), mksem("sW2b")]
    sO = mksem("sO")
    sT = mksem("sT")
    sV = mksem("sV")
    sS = mksem("sS")

    def tap_sync():
        sch.wait("y", sO, sO.c)
        sch.wait("v", sO, sO.c)
        sch.wait("s", sO, sO.c)

    def _replay():
        with nc.Block() as block:
            block.tensor(lambda e: [f(e) for f in sch.q["t"]])
            block.vector(lambda e: [f(e) for f in sch.q["v"]])
            block.scalar(lambda e: [f(e) for f in sch.q["s"]])
            block.gpsimd(lambda e: [f(e) for f in sch.q["g"]])
            block.sync(lambda e: [f(e) for f in sch.q["y"]])
        return nc

    def _finish_early():
        sch.wait("y", sV, sV.c)
        sch.wait("y", sS, sS.c)
        sch.wait("y", sT, sT.c)
        for ct in range(KT):
            sch.op("y", lambda e, ct=ct: e.dma_start(
                out=y_d[ct], in_=x32[:, ct * L:(ct + 1) * L]), inc=sO, n=16)
        sch.wait("y", sO, sO.c)
        return _replay()

    x32, _ = alloc("x32", [128, KT * L])
    vecs, _ = alloc("vecs", [128, 72])
    modv, _ = alloc("modv", [128, 56])
    cv32, _ = alloc("cv32", [128, 8])
    sc8, _ = alloc("sc8", [128, 32])
    ones_m, _ = alloc("ones_m", [128, 1], BF16)
    ones_r, _ = alloc("ones_r", [1, 128], BF16)
    gw_s, _ = alloc("gw_s", [128, 64], BF16)
    cst, _ = alloc("cst", [128, 8])

    wT_s, _ = alloc("wT_s", [8, L])
    hbf, cm_hb = alloc("hb", [128, KT * L], BF16)
    hffn = hbf
    meanB, cm_meanB = alloc("meanB", [128, L])
    rstdB, cm_rstdB = alloc("rstdB", [128, L])
    rwA, cm_rwA = alloc("rwA", [1, SCW])
    rwB, cm_rwB = alloc("rwB", [1, SCW])
    rwbf, cm_rwbf = alloc("rwbf", [1, 4 * SCW], BF16)
    chA, cm_chA = alloc("chA", [128, 4 * SCW], BF16)
    wmb, cm_wmb = alloc("wmb", [128, 4 * 3072])
    mrow, cm_mrow = alloc("mrow", [1, 3072])

    PS0, _ = palloc("PS0", [128, 2048])
    PS1, _ = palloc("PS1", [128, 2048])

    def EP(i):
        return E[:, i * L:(i + 1) * L]

    def col(t, j):
        return t[:, j:j + 1]

    V_CONVB, V_WRB, V_WIB, V_LOGA, V_NORMW, V_CONVW = 0, 8, 16, 24, 32, 40
    M_SA, M_CA1, M_GAC, M_SB, M_CB1, M_GBC, M_LA8 = 0, 8, 16, 24, 32, 40, 48

    for k in range(KT):
        sch.op("y", lambda e, k=k: e.dma_start(out=x32[:, k * L:(k + 1) * L], in_=x_d[k]),
               inc=sX, n=16)
    x_loaded = sX.c
    for dst, src in ((cv32[:], cvec_d),
                     (vecs[:, 0:8], convb_d), (vecs[:, 8:16], wrb_d),
                     (vecs[:, 16:24], wib_d), (vecs[:, 24:32], loga_d),
                     (vecs[:, 32:40], normw_d), (vecs[:, 40:72], convw_d)):
        sch.op("y", lambda e, dst=dst, src=src: e.dma_start(out=dst, in_=src), inc=sXV, n=16)
    vecs_ld = sXV.c
    sch.op("g", lambda e: e.dma_start(
        out=gw_s[:].rearrange("p (k e) -> p k e", k=8),
        in_=gwt_d.rearrange("k p e -> p k e")), inc=sGW, n=16)
    gw_ld = sGW.c
    wm_ld = {}
    wm_used = {}

    def issue_wm(half, k):
        prevk = (half, k - 4) if k >= 4 else ((half - 1, KT - 4 + k) if half > 0 else None)
        if prevk is not None and prevk in wm_used:
            sch.wait("g" if k % 2 == 0 else "y", sT, wm_used[prevk])
        q = "g" if k % 2 == 0 else "y"
        sch.op(q, lambda e, h=half, k=k: e.dma_start(
            out=wmb[:, (k % 4) * 3072:(k % 4) * 3072 + 3072],
            in_=wmod_d[k][:, h * 6:(h + 1) * 6, :].rearrange("p a b -> p (a b)")),
            inc=sWM[k % 4], n=16)
        wm_ld[(half, k)] = (sWM[k % 4], sWM[k % 4].c)
    for _k in range(4):
        issue_wm(0, _k)

    sch.op("v", lambda e: e.memset(ones_m[:], 1.0 / 1024.0))
    sch.op("v", lambda e: e.memset(ones_r[:], 1.0))
    sch.op("v", lambda e: e.memset(cst[:, 0:1], 1e-5))
    sch.op("v", lambda e: e.memset(cst[:, 1:2], 1e-6))
    sch.op("v", lambda e: e.memset(cst[:, 2:3], -0.5))
    sch.op("v", lambda e: e.memset(cst[:, 3:4], -1.0))
    sch.op("v", lambda e: e.memset(cst[:, 4:5], 1.0))
    sch.op("v", lambda e: e.memset(cst[:, 5:6], float(np.float32(1.0) - np.float32(1e-6))))
    sch.op("v", lambda e: e.memset(
        cst[:, 6:7], float(np.float32(1.0) - (np.float32(1.0) - np.float32(1e-6)))))
    v_cst = sch.op("v", lambda e: e.memset(cst[:, 7:8], 0.5), inc=sV)
    sch.drain("v")

    sch.wait("s", sXV, vecs_ld)
    sch.wait("s", sV, v_cst)
    s_silu = sch.op("s", lambda e: e.activation(cv32[:], cv32[:], AF.Silu), inc=sS)

    sch.drain("s")
    sch.op("s", lambda e: e.activation(sc8[:, 8:16], vecs[:, 24:32], AF.Sigmoid))
    sch.drain("s")
    s_ln = sch.op("s", lambda e: e.activation(sc8[:, 8:16], sc8[:, 8:16], AF.Ln), inc=sS)
    sch.wait("v", sS, s_ln)
    sch.drain("v")
    v_la8 = sch.op("v", lambda e: e.tensor_scalar(
        modv[:, M_LA8:M_LA8 + 8], sc8[:, 8:16], 8.0, None, OP.mult), inc=sV)

    sch.wait("t", sS, s_silu)
    mod_bank_free = [0, 0]
    mrow_free = 0
    tr_idx = 0
    for half in range(2):
        t_row = None
        for k in range(KT):
            sch.wait("t", *wm_ld[(half, k)])
            if k == 0 and mrow_free:
                sch.wait("t", sV, mrow_free)
            for nc_ in range(6):
                ps = PS0 if nc_ < 4 else PS1
                t_row = sch.op("t", lambda e, k=k, nc_=nc_, ps=ps: e.matmul(
                    ps[0:1, (nc_ % 4) * 512:(nc_ % 4) * 512 + 512],
                    cv32[:, k:k + 1],
                    wmb[:, (k % 4) * 3072 + nc_ * 512:(k % 4) * 3072 + nc_ * 512 + 512],
                    start=(k == 0), stop=(k == KT - 1)),
                    inc=sT if nc_ == 5 else None)
            wm_used[(half, k)] = t_row
            nk = (half, k + 4) if k + 4 < KT else ((half + 1, k + 4 - KT) if half == 0 else None)
            if nk is not None:
                issue_wm(*nk)
        sch.wait("v", sT, t_row)
        v_row = None
        for nc_ in range(6):
            ps = PS0 if nc_ < 4 else PS1
            v_row = sch.op("v", lambda e, nc_=nc_, ps=ps: e.tensor_copy(
                mrow[0:1, nc_ * 512:(nc_ + 1) * 512],
                ps[0:1, (nc_ % 4) * 512:(nc_ % 4) * 512 + 512]), inc=sV)
        mrow_free = v_row
        sch.wait("t", sV, v_row)
        if mod_bank_free[half % 2]:
            sch.wait("t", sV, mod_bank_free[half % 2])
        t_tr = None
        for blk in range(24):
            t_tr = sch.op("t", lambda e, blk=blk, half=half: e.matmul(
                PS1[:, (2 + half) * 512 + blk:(2 + half) * 512 + blk + 1],
                mrow[0:1, blk * 128:(blk + 1) * 128],
                cst[0:1, 4:5],
                start=(blk == 0), stop=True,
                skip_group_check=True), inc=sT if blk == 23 else None)
        sch.wait("v", sT, t_tr)
        mod_bank_free[half % 2] = sch.op(
            "v", lambda e, half=half: e.tensor_copy(
                modv[:, half * 24:(half + 1) * 24],
                PS1[:, (2 + half) * 512:(2 + half) * 512 + 24]), inc=sV)
    mod_done = sV.c
    mod_t_done = sT.c

    sch.drain("v")
    sch.op("v", lambda e: e.tensor_scalar(modv[:, 8:16], modv[:, 8:16], 1.0, None, OP.add))
    sch.op("v", lambda e: e.tensor_scalar(modv[:, 32:40], modv[:, 32:40], 1.0, None, OP.add))
    for GC in (M_GAC, M_GBC):
        sch.drain("v")
        sch.op("v", lambda e, G=GC: e.tensor_scalar(sc8[:, 0:8], modv[:, G:G + 8], 0.0, None, OP.is_ge))
        sch.op("v", lambda e, G=GC: e.tensor_scalar(sc8[:, 8:16], modv[:, G:G + 8], 0.25, None, OP.max))
        sch.op("v", lambda e, G=GC: e.tensor_scalar(sc8[:, 16:24], modv[:, G:G + 8], -0.25, None, OP.min))
        sch.drain("v")
        sch.op("v", lambda e: e.tensor_tensor(sc8[:, 8:16], sc8[:, 8:16], sc8[:, 16:24], OP.subtract))
        sch.drain("v")
        sch.op("v", lambda e: e.tensor_tensor(sc8[:, 8:16], sc8[:, 8:16], sc8[:, 0:8], OP.mult))
        sch.drain("v")
        sch.op("v", lambda e, G=GC: e.tensor_tensor(modv[:, G:G + 8], sc8[:, 16:24], sc8[:, 8:16], OP.add))
    sch.drain("v")
    mod_post = sch.op("v", lambda e: e.tensor_copy(sc8[:, 0:1], modv[:, 0:1]), inc=sV)
    if "mod" in tap_d:
        sch.wait("y", sV, mod_post)
        sch.op("y", lambda e: e.dma_start(out=tap_d["mod"], in_=modv[:]), inc=sO, n=16)
        tap_sync()

    if phase_limit <= 0:
        return _finish_early()
    cm_mrow.__exit__(None, None, None)
    cm_wmb.__exit__(None, None, None)
    wsl, cm_wsl = alloc("wsl", [128, 3 * 1024], BF16)
    xcbf, cm_xcbf = alloc("xcbf", [128, KT * L], BF16)
    E, cm_E = alloc("E", [128, 3 * L])
    hlnb, cm_hlnb = alloc("hlnb", [128, 2 * L], BF16)
    cvd, cm_cvd = alloc("cvd", [128, 2 * 5 * 128], BF16)
    onesc, cm_onesc = alloc("onesc", [128, 512], BF16)

    def emit_stats(src, eps, want_mean):
        RH = rwbf[0:1, 0 * SCW:1 * SCW]
        RL = rwbf[0:1, 1 * SCW:2 * SCW]
        MH = rwbf[0:1, 2 * SCW:3 * SCW]
        ML = rwbf[0:1, 3 * SCW:4 * SCW]
        eps_col = cst[0:1, 0:1] if eps == 1e-5 else cst[0:1, 1:2]
        mm_done = []
        nch = 0
        for c in range(NSC):
            for k in range(KT):
                par = nch % 2
                a_sl = chA[:, par * SCW:(par + 1) * SCW]
                b_sl = chA[:, (2 + par) * SCW:(3 + par) * SCW]
                if nch >= 2:
                    sch.wait("s", sT, mm_done[nch - 2])
                if want_mean:
                    sch.op("s", lambda e, a=a_sl, k=k, c=c: e.copy(a, src(k, c)))
                    sch.drain("s")
                s_sq = sch.op("s", lambda e, b=b_sl, k=k, c=c: e.activation(
                    b, src(k, c), AF.Square), inc=sS)
                sch.wait("t", sS, s_sq)
                if want_mean:
                    sch.op("t", lambda e, a=a_sl, c=c, k=k: e.matmul(
                        PS1[0:1, (c % 4) * 512:(c % 4) * 512 + SCW], ones_m[:], a,
                        start=(k == 0), stop=(k == KT - 1)))
                tm = sch.op("t", lambda e, b=b_sl, c=c, k=k: e.matmul(
                    PS0[0:1, (c % 4) * 512:(c % 4) * 512 + SCW], ones_m[:], b,
                    start=(k == 0), stop=(k == KT - 1)), inc=sT)
                mm_done.append(tm)
                nch += 1
        ret = None
        prev_spl = None
        prev_bc = None
        for c in range(NSC):
            msq_row = PS0[0:1, (c % 4) * 512:(c % 4) * 512 + SCW]
            mean_row = PS1[0:1, (c % 4) * 512:(c % 4) * 512 + SCW]
            sch.wait("s", sT, mm_done[c * KT + KT - 1])
            if prev_spl is not None:
                sch.wait("s", sV, prev_spl)
            sch.drain("s")
            if want_mean:
                s_m2 = sch.op("s", lambda e, m=mean_row: e.activation(rwA[:], m, AF.Square),
                              inc=sS)
                sch.wait("v", sS, s_m2)
                sch.drain("v")
                v_var = sch.op("v", lambda e, q=msq_row: e.tensor_tensor(
                    rwB[:], q, rwA[:], OP.subtract), inc=sV)
                sch.wait("s", sV, v_var)
            else:
                s_cp = sch.op("s", lambda e, q=msq_row: e.copy(rwB[:], q), inc=sS)
                sch.wait("s", sS, s_cp)
            sch.drain("s")
            sch.op("s", lambda e: e.activation(rwB[:], rwB[:], AF.Ln, bias=eps_col))
            sch.drain("s")
            s_rstd = sch.op("s", lambda e: e.activation(
                rwB[:], rwB[:], AF.Exp, scale=cst[0:1, 2:3]), inc=sS)
            sch.wait("v", sS, s_rstd)
            if prev_bc is not None:
                sch.wait("v", sT, prev_bc)
            sch.drain("v")
            sch.op("v", lambda e, RH=RH: e.tensor_copy(RH, rwB[:]))
            sch.drain("v")
            sch.op("v", lambda e, RH=RH: e.tensor_tensor(rwB[:], rwB[:], RH, OP.subtract))
            sch.drain("v")
            v_spl = sch.op("v", lambda e, RL=RL: e.tensor_copy(RL, rwB[:]), inc=sV)
            if want_mean:
                sch.op("v", lambda e, MH=MH, m=mean_row: e.tensor_copy(MH, m))
                sch.drain("v")
                sch.op("v", lambda e, MH=MH, m=mean_row: e.tensor_tensor(rwA[:], m, MH, OP.subtract))
                sch.drain("v")
                v_spl = sch.op("v", lambda e, ML=ML: e.tensor_copy(ML, rwA[:]), inc=sV)
            sch.wait("t", sV, v_spl)
            sch.op("t", lambda e, c=c, RH=RH: e.matmul(
                PS0[:, (c % 4) * 512:(c % 4) * 512 + SCW], ones_r[:], RH,
                start=True, stop=False))
            t_bc = sch.op("t", lambda e, c=c, RL=RL: e.matmul(
                PS0[:, (c % 4) * 512:(c % 4) * 512 + SCW], ones_r[:], RL,
                start=False, stop=True), inc=sT)
            if want_mean:
                sch.op("t", lambda e, c=c, MH=MH: e.matmul(
                    PS1[:, (c % 4) * 512:(c % 4) * 512 + SCW], ones_r[:], MH,
                    start=True, stop=False))
                t_bc = sch.op("t", lambda e, c=c, ML=ML: e.matmul(
                    PS1[:, (c % 4) * 512:(c % 4) * 512 + SCW], ones_r[:], ML,
                    start=False, stop=True), inc=sT)
            sch.wait("v", sT, t_bc)
            ret = sch.op("v", lambda e, c=c: e.tensor_copy(
                rstdB[:, c * SCW:(c + 1) * SCW],
                PS0[:, (c % 4) * 512:(c % 4) * 512 + SCW]), inc=sV)
            if want_mean:
                ret = sch.op("v", lambda e, c=c: e.tensor_copy(
                    meanB[:, c * SCW:(c + 1) * SCW],
                    PS1[:, (c % 4) * 512:(c % 4) * 512 + SCW]), inc=sV)
            prev_spl = ret
            prev_bc = t_bc
        return ret

    sch.wait("s", sX, x_loaded)
    sch.wait("t", sV, mod_done)
    sch.wait("v", sV, v_la8)
    ln1 = emit_stats(lambda k, c: x32[:, k * L + c * SCW: k * L + c * SCW + SCW], 1e-5, True)
    sch.wait("v", sV, ln1)
    sch.wait("s", sV, ln1)
    cv_ld = {}
    cv_used = {}

    def issue_cv(ct):
        if ct >= 2 and (ct - 2) in cv_used:
            sch.wait("g", sT, cv_used[ct - 2])
        sch.op("g", lambda e, ct=ct: e.dma_start(
            out=cvd[:, (ct % 2) * 640:(ct % 2) * 640 + 640],
            in_=convd_d[ct].rearrange("p a n -> p (a n)")), inc=sCV[ct % 2], n=16)
        cv_ld[ct] = (sCV[ct % 2], sCV[ct % 2].c)
    sch.wait("g", sT, mod_t_done)
    issue_cv(0)
    issue_cv(1)
    sch.drain("v")
    v_onesc = sch.op("v", lambda e: e.memset(onesc[:], 1.0), inc=sV)
    sch.drain("v")
    sch.wait("t", sV, v_onesc)
    xcb = []
    xc_mm = {}
    hln_s = {}
    for ct in range(KT):
        par = ct % 2
        if ct > 0:
            sch.wait("v", sS, hln_s[ct - 1])
        sch.drain("v")
        sch.op("v", lambda e, ct=ct: e.tensor_tensor(
            EP(0), x32[:, ct * L:(ct + 1) * L], meanB[:], OP.subtract))
        sch.drain("v")
        v1 = sch.op("v", lambda e, ct=ct: e.scalar_tensor_tensor(
            EP(0), EP(0), col(modv, M_CA1 + ct), rstdB[:], OP.mult, OP.mult), inc=sV)
        sch.wait("s", sV, v1)
        if ct >= 2:
            sch.wait("s", sT, xc_mm[ct - 2])
        sch.drain("s")
        s1 = sch.op("s", lambda e, ct=ct, par=par: e.activation(
            hlnb[:, par * L:(par + 1) * L], EP(0), AF.Identity,
            bias=col(modv, M_SA + ct)), inc=sS)
        hln_s[ct] = s1
        ps = PS0 if par == 0 else PS1
        sch.wait("t", sS, s1)
        sch.wait("t", *cv_ld[ct])
        if ct >= 2:
            sch.wait("t", sS, xcb[ct - 2])
        elif ct == 0:
            sch.wait("t", sV, ln1)
        t_cv = None
        for c in range(NSC):
            c0, c1 = c * 512, min((c + 1) * 512, L)
            t_cv = sch.op("t", lambda e, c0=c0, c1=c1, par=par, ps=ps: e.matmul(
                ps[:, c0:c1], cvd[:, par * 640 + 512: par * 640 + 640],
                onesc[:, 0:c1 - c0], start=True, stop=False, skip_group_check=True))
            for kk in range(4):
                sh = 3 - kk
                lo = max(sh, c0)
                if lo >= c1:
                    continue
                t_cv = sch.op("t", lambda e, kk=kk, lo=lo, c1=c1, sh=sh, par=par, ps=ps: e.matmul(
                    ps[:, lo:c1],
                    cvd[:, par * 640 + kk * 128: par * 640 + kk * 128 + 128],
                    hlnb[:, par * L + lo - sh: par * L + c1 - sh],
                    start=False, stop=(kk == 3), skip_group_check=True),
                    inc=sT if (kk == 3 and c1 == L) else None)
        xc_mm[ct] = t_cv
        cv_used[ct] = t_cv
        if ct + 2 < KT:
            issue_cv(ct + 2)
        sch.wait("s", sT, t_cv)
        sch.drain("s")
        s2 = sch.op("s", lambda e, ct=ct, ps=ps: e.copy(
            xcbf[:, ct * L:(ct + 1) * L], ps[:, 0:L]), inc=sS)
        xcb.append(s2)
        if "xc" in tap_d:
            sch.wait("y", sS, s2)
            sch.drain("s")
            s_tp = sch.op("s", lambda e, ct=ct: e.copy(EP(0), xcbf[:, ct * L:(ct + 1) * L]), inc=sS)
            sch.wait("y", sS, s_tp)
            sch.op("y", lambda e, ct=ct: e.dma_start(out=tap_d["xc"][ct], in_=EP(0)), inc=sO, n=16)
            tap_sync()
    xcbf_all = xcb[-1]

    wr_ld, wi_ld, ow_ld = {}, {}, {}

    def slice_dma(slot, src, ld_map, key):
        sch.op("g", lambda e, d=slot, s=src: e.dma_start(
            out=wsl[:, d * 1024:(d + 1) * 1024], in_=s.rearrange("p a b -> p (a b)")),
            inc=sWS[slot], n=16)
        ld_map[key] = (sWS[slot], sWS[slot].c)

    sch.wait("g", sT, mod_t_done)
    slice_dma(0, wr_d[0], wr_ld, 0)
    slice_dma(2, wi_d[0], wi_ld, 0)
    slice_dma(1, wr_d[1], wr_ld, 1)

    r_free = i_free = None
    vh = None
    vh_prev = None
    sch.wait("v", sS, xcbf_all)
    for ct in range(KT):
        sch.wait("t", sS, xcbf_all)
        sch.wait("t", *wr_ld[ct])
        if r_free:
            sch.wait("t", sS, r_free[0])
            sch.wait("t", sV, r_free[1])
        for c in range(NSC):
            for k in range(KT):
                t_r = sch.op("t", lambda e, ct=ct, c=c, k=k: e.matmul(
                    PS0[:, c * 512: c * 512 + SCW],
                    wsl[:, (ct % 2) * 1024 + k * 128:(ct % 2) * 1024 + k * 128 + 128],
                    xcbf[:, k * L + c * SCW: k * L + c * SCW + SCW],
                    start=(k == 0), stop=(k == KT - 1)),
                    inc=sT if (c == NSC - 1 and k == KT - 1) else None)
        sch.wait("t", *wi_ld[ct])
        if i_free:
            sch.wait("t", sV, i_free)
        for c in range(NSC):
            for k in range(KT):
                t_i = sch.op("t", lambda e, ct=ct, c=c, k=k: e.matmul(
                    PS1[:, c * 512: c * 512 + SCW],
                    wsl[:, 2 * 1024 + k * 128: 2 * 1024 + k * 128 + 128],
                    xcbf[:, k * L + c * SCW: k * L + c * SCW + SCW],
                    start=(k == 0), stop=(k == KT - 1)),
                    inc=sT if (c == NSC - 1 and k == KT - 1) else None)
        if ct + 2 <= KT - 1:
            sch.wait("g", sT, t_r)
            slice_dma(ct % 2, wr_d[ct + 2], wr_ld, ct + 2)
        if ct + 1 <= KT - 1:
            sch.wait("g", sT, t_i)
            slice_dma(2, wi_d[ct + 1], wi_ld, ct + 1)
        sch.wait("s", sT, t_r)
        sch.drain("s")
        s_rs = sch.op("s", lambda e, ct=ct: e.activation(
            PS0[:, 0:L], PS0[:, 0:L], AF.Sigmoid, bias=col(vecs, V_WRB + ct)), inc=sS)
        sch.wait("s", sT, t_i)
        sch.drain("s")
        s_is = sch.op("s", lambda e, ct=ct: e.activation(
            PS1[:, 0:L], PS1[:, 0:L], AF.Sigmoid, bias=col(vecs, V_WIB + ct)), inc=sS)
        if "r" in tap_d:
            if vh_prev:
                sch.wait("s", sV, vh_prev)
            s_tp = sch.op("s", lambda e: e.copy(EP(0), PS0[:, 0:L]), inc=sS)
            sch.wait("y", sS, s_tp)
            sch.op("y", lambda e, ct=ct: e.dma_start(out=tap_d["r"][ct], in_=EP(0)), inc=sO, n=16)
            tap_sync()
        if vh_prev:
            sch.wait("s", sV, vh_prev)
        sch.drain("s")
        s_a = sch.op("s", lambda e, ct=ct: e.activation(
            EP(1), PS0[:, 0:L], AF.Exp, scale=col(modv, M_LA8 + ct)), inc=sS)
        sch.wait("v", sS, s_rs)
        sch.drain("v")
        v_la = sch.op("v", lambda e, ct=ct: e.tensor_scalar(
            EP(0), PS0[:, 0:L], col(modv, M_LA8 + ct), LOG1EM7, OP.mult, OP.max), inc=sV)
        r_free = (s_a, v_la)
        sch.drain("v")
        sch.op("v", lambda e: e.tensor_tensor_scan(EP(0), EP(0), EP(0), 0.0, OP.add, OP.bypass))
        sch.drain("v")
        v_cl = sch.op("v", lambda e: e.tensor_scalar(EP(0), EP(0), -80.0, None, OP.max), inc=sV)
        sch.drain("s")
        sch.op("s", lambda e: e.activation(EP(1), EP(1), AF.Square))
        sch.drain("s")
        sch.op("s", lambda e: e.activation(EP(1), EP(1), AF.Relu,
                                           scale=cst[:, 3:4], bias=cst[:, 5:6]))
        sch.drain("s")
        sch.op("s", lambda e: e.activation(EP(1), EP(1), AF.Ln, bias=cst[:, 6:7]))
        sch.drain("s")
        s_b0 = sch.op("s", lambda e: e.activation(
            EP(1), EP(1), AF.Exp, scale=cst[:, 7:8]), inc=sS)
        sch.wait("v", sS, s_b0)
        sch.drain("v")
        v_bi = sch.op("v", lambda e: e.tensor_tensor(EP(2), EP(1), PS1[:, 0:L], OP.mult), inc=sV)
        i_free = v_bi
        sch.drain("v")
        v_bxc = sch.op("v", lambda e, ct=ct: e.tensor_tensor(
            EP(2), EP(2), xcbf[:, ct * L:(ct + 1) * L], OP.mult), inc=sV)
        sch.wait("s", sV, v_bxc)
        sch.drain("s")
        s_A = sch.op("s", lambda e: e.activation(EP(1), EP(0), AF.Exp), inc=sS)
        sch.drain("s")
        s_u = sch.op("s", lambda e: e.activation(EP(0), EP(0), AF.Exp, scale=cst[:, 3:4]), inc=sS)
        sch.wait("v", sS, s_u)
        sch.drain("v")
        sch.op("v", lambda e: e.tensor_tensor(EP(2), EP(2), EP(0), OP.mult))
        sch.drain("v")
        sch.op("v", lambda e: e.tensor_tensor_scan(EP(2), EP(2), EP(2), 0.0, OP.add, OP.bypass))
        sch.drain("v")
        vh = sch.op("v", lambda e, ct=ct: e.tensor_tensor(
            hbf[:, ct * L:(ct + 1) * L], EP(1), EP(2), OP.mult), inc=sV)
        vh_prev = vh
        if "h" in tap_d:
            sch.wait("s", sV, vh)
            sch.drain("s")
            s_tap = sch.op("s", lambda e, ct=ct: e.copy(EP(2), hbf[:, ct * L:(ct + 1) * L]), inc=sS)
            sch.wait("y", sS, s_tap)
            sch.op("y", lambda e, ct=ct: e.dma_start(out=tap_d["h"][ct], in_=EP(2)), inc=sO, n=16)
            tap_sync()
    h_all = vh

    if phase_limit <= 2:
        return _finish_early()
    cm_onesc.__exit__(None, None, None)
    cm_cvd.__exit__(None, None, None)
    cm_hlnb.__exit__(None, None, None)
    cm_E.__exit__(None, None, None)
    cm_xcbf.__exit__(None, None, None)
    y32, cm_y32 = alloc("y32", [128, KT * L])

    sch.wait("g", sT, sT.c)
    slice_dma(0, ow_d[0], ow_ld, 0)
    slice_dma(1, ow_d[1], ow_ld, 1)
    y_ev = {}
    sch.wait("t", sV, h_all)
    sch.wait("t", sS, r_free[0])
    sch.wait("t", sV, r_free[1])
    sch.wait("t", sV, i_free)
    for do in range(KT):
        sch.wait("t", *ow_ld[do])
        if do >= 2:
            sch.wait("t", sV, y_ev[do - 2])
        ps = PS0 if do % 2 == 0 else PS1
        t_y = None
        for c in range(NSC):
            for k in range(KT):
                t_y = sch.op("t", lambda e, do=do, c=c, k=k, ps=ps: e.matmul(
                    ps[:, c * 512: c * 512 + SCW],
                    wsl[:, (do % 2) * 1024 + k * 128:(do % 2) * 1024 + k * 128 + 128],
                    hbf[:, k * L + c * SCW: k * L + c * SCW + SCW],
                    start=(k == 0), stop=(k == KT - 1)),
                    inc=sT if (c == NSC - 1 and k == KT - 1) else None)
        if do + 2 < KT:
            sch.wait("g", sT, t_y)
            slice_dma(do % 2, ow_d[do + 2], ow_ld, do + 2)
        sch.wait("v", sT, t_y)
        y_ev[do] = sch.op("v", lambda e, do=do, ps=ps: e.tensor_copy(
            y32[:, do * L:(do + 1) * L], ps[:, 0:L]), inc=sV)
    if "y" in tap_d:
        sch.wait("y", sV, y_ev[KT - 1])
        for k in range(KT):
            sch.op("y", lambda e, k=k: e.dma_start(
                out=tap_d["y"][k], in_=y32[:, k * L:(k + 1) * L]), inc=sO, n=16)
        tap_sync()
    sch.wait("s", sV, y_ev[KT - 1])
    sch.wait("t", sV, y_ev[KT - 1])
    rms = emit_stats(lambda k, c: y32[:, k * L + c * SCW: k * L + c * SCW + SCW], 1e-6, False)
    sch.wait("v", sV, rms)
    if "rstd" in tap_d:
        sch.wait("y", sV, rms)
        sch.op("y", lambda e: e.dma_start(out=tap_d["rstd"], in_=rstdB[:]), inc=sO, n=16)
        tap_sync()
    vr = None
    for ct in range(KT):
        sch.drain("v")
        sch.op("v", lambda e, ct=ct: e.scalar_tensor_tensor(
            y32[:, ct * L:(ct + 1) * L], y32[:, ct * L:(ct + 1) * L],
            col(vecs, V_NORMW + ct), rstdB[:], OP.mult, OP.mult))
        sch.drain("v")
        vr = sch.op("v", lambda e, ct=ct: e.scalar_tensor_tensor(
            x32[:, ct * L:(ct + 1) * L], y32[:, ct * L:(ct + 1) * L], col(modv, M_GAC + ct),
            x32[:, ct * L:(ct + 1) * L], OP.mult, OP.add), inc=sV)
    xattn = vr
    if "xattn" in tap_d:
        sch.wait("y", sV, xattn)
        for k in range(KT):
            sch.op("y", lambda e, k=k: e.dma_start(
                out=tap_d["xattn"][k], in_=x32[:, k * L:(k + 1) * L]), inc=sO, n=16)
        tap_sync()

    if phase_limit <= 3:
        return _finish_early()
    cm_y32.__exit__(None, None, None)
    cm_wsl.__exit__(None, None, None)
    scr4, cm_scr4 = alloc("scr4", [128, L])
    eye_s, cm_eye = alloc("eye_s", [128, 128])
    sch.wait("y", sV, xattn)
    sch.op("y", lambda e: e.dma_start(out=eye_s[:], in_=eye_d), inc=sEY, n=16)
    eye_ld = sEY.c
    scoS, cm_scoS = alloc("scoS", [128, TT * 8])
    wtok, cm_wtok = alloc("wtok", [128, TT * 8])
    rts, cm_rts = alloc("rts", [128, TT * 6])

    sch.wait("s", sV, xattn)
    sch.wait("t", sV, rms)
    ln2 = emit_stats(lambda k, c: x32[:, k * L + c * SCW: k * L + c * SCW + SCW], 1e-5, True)
    sch.wait("v", sV, ln2)
    hf = None
    hf_ct = []
    for ct in range(KT):
        sch.op("v", lambda e, ct=ct: e.tensor_tensor(
            scr4[:], x32[:, ct * L:(ct + 1) * L], meanB[:], OP.subtract))
        sch.drain("v")
        vln = sch.op("v", lambda e, ct=ct: e.scalar_tensor_tensor(
            scr4[:], scr4[:], col(modv, M_CB1 + ct), rstdB[:], OP.mult, OP.mult), inc=sV)
        sch.wait("s", sV, vln)
        sch.drain("s")
        hf = sch.op("s", lambda e, ct=ct: e.activation(
            hffn[:, ct * L:(ct + 1) * L], scr4[:], AF.Identity,
            bias=col(modv, M_SB + ct)), inc=sS)
        hf_ct.append(hf)
        sch.wait("v", sS, hf)
    hffn_all = hf
    HFFN_SEM = "s"
    if "hffn" in tap_d:
        for k in range(KT):
            sch.wait("s", sS, hffn_all) if False else None
            sch.drain("s")
            s_tap = sch.op("s", lambda e, k=k: e.copy(scr4[:], hffn[:, k * L:(k + 1) * L]), inc=sS)
            sch.wait("y", sS, s_tap)
            sch.op("y", lambda e, k=k: e.dma_start(out=tap_d["hffn"][k], in_=scr4[:]), inc=sO, n=16)
            tap_sync()

    sch.wait("t", sGW, gw_ld)
    sch.wait("t", sV, ln2)
    t_sc = None
    for k in range(KT):
        sch.wait("t", sS, hf_ct[k])
        for tt in range(TT):
            t_sc = sch.op("t", lambda e, tt=tt, k=k: e.matmul(
                PS0[:, tt * 8:(tt + 1) * 8],
                hffn[:, k * L + tt * 128: k * L + tt * 128 + 128],
                gw_s[:, k * 8:(k + 1) * 8],
                start=(k == 0 and tt == 0), stop=(k == KT - 1),
                skip_group_check=True),
                inc=sT if (k == KT - 1 and tt == TT - 1) else None)
    sch.wait("v", sT, t_sc)
    sch.drain("v")
    sco_all = sch.op("v", lambda e: e.tensor_copy(scoS[:], PS0[:, 0:TT * 8]), inc=sV)
    sch.wait("v", sV, sco_all)
    R_M, R_SUM, R_M1, R_M2, R_DEN = 0, 1, 2, 3, 4

    def rcol(q, tt):
        return rts[:, q * TT + tt: q * TT + tt + 1]

    def sco(tt):
        return scoS[:, tt * 8:(tt + 1) * 8]

    def wtk(tt):
        return wtok[:, tt * 8:(tt + 1) * 8]

    sch.drain("v")
    for tt in range(TT):
        sch.op("v", lambda e, tt=tt: e.reduce_max(rcol(R_M, tt), sco(tt), axis=AX.X))
    sch.drain("v")
    v_sh = None
    for tt in range(TT):
        v_sh = sch.op("v", lambda e, tt=tt: e.tensor_scalar(
            wtk(tt), sco(tt), rcol(R_M, tt), None, OP.subtract),
            inc=sV if tt == TT - 1 else None)
    sch.wait("s", sV, v_sh)
    sch.drain("s")
    s_pe = None
    for tt in range(TT):
        s_pe = sch.op("s", lambda e, tt=tt: e.activation(sco(tt), wtk(tt), AF.Exp),
                      inc=sS if tt == TT - 1 else None)
    sch.wait("v", sS, s_pe)
    for tt in range(TT):
        sch.op("v", lambda e, tt=tt: e.tensor_reduce(rcol(R_SUM, tt), sco(tt), axis=AX.X, op=OP.add))
    sch.drain("v")
    for tt in range(TT):
        sch.op("v", lambda e, tt=tt: e.reciprocal(rcol(R_DEN, tt), rcol(R_SUM, tt)))
    sch.drain("v")
    for tt in range(TT):
        sch.op("v", lambda e, tt=tt: e.tensor_scalar(sco(tt), sco(tt), rcol(R_DEN, tt), None, OP.mult))
    sch.drain("v")
    if "probs" in tap_d:
        v_pr = sch.op("v", lambda e: e.tensor_copy(sc8[:, 2:3], scoS[:, 0:1]), inc=sV)
        sch.wait("y", sV, v_pr)
        sch.op("y", lambda e: e.dma_start(out=tap_d["probs"], in_=scoS[:]), inc=sO, n=16)
        tap_sync()
    for tt in range(TT):
        sch.op("v", lambda e, tt=tt: e.reduce_max(rcol(R_M1, tt), sco(tt), axis=AX.X))
    sch.drain("v")
    for tt in range(TT):
        sch.op("v", lambda e, tt=tt: e.tensor_scalar(wtk(tt), sco(tt), rcol(R_M1, tt), None, OP.is_ge))
    sch.drain("v")
    for tt in range(TT):
        sch.op("v", lambda e, tt=tt: e.scalar_tensor_tensor(wtk(tt), wtk(tt), -2.0, sco(tt), OP.mult, OP.add))
    sch.drain("v")
    for tt in range(TT):
        sch.op("v", lambda e, tt=tt: e.reduce_max(rcol(R_M2, tt), wtk(tt), axis=AX.X))
    sch.drain("v")
    for tt in range(TT):
        sch.op("v", lambda e, tt=tt: e.tensor_scalar(wtk(tt), sco(tt), rcol(R_M2, tt), None, OP.is_ge))
    for tt in range(TT):
        sch.op("v", lambda e, tt=tt: e.tensor_tensor(rcol(R_DEN, tt), rcol(R_M1, tt), rcol(R_M2, tt), OP.add))
    sch.drain("v")
    for tt in range(TT):
        sch.op("v", lambda e, tt=tt: e.reciprocal(rcol(R_M, tt), rcol(R_DEN, tt)))
    sch.drain("v")
    for tt in range(TT):
        sch.op("v", lambda e, tt=tt: e.tensor_tensor(wtk(tt), wtk(tt), sco(tt), OP.mult))
    sch.drain("v")
    v_w = None
    for tt in range(TT):
        v_w = sch.op("v", lambda e, tt=tt: e.tensor_scalar(
            wtk(tt), wtk(tt), rcol(R_M, tt), None, OP.mult),
            inc=sV if tt == TT - 1 else None)
    if "wsel" in tap_d:
        sch.wait("y", sV, v_w)
        sch.op("y", lambda e: e.dma_start(out=tap_d["wsel"], in_=wtok[:]), inc=sO, n=16)
        tap_sync()
    sch.wait("t", sV, v_w)
    sch.wait("t", sEY, eye_ld)
    wt_ev = {}
    for tt in range(TT):
        if tt >= 4:
            sch.wait("t", sV, wt_ev[tt - 4])
        t_tr = sch.op("t", lambda e, tt=tt: e.transpose(
            PS1[0:8, (tt % 4) * 512:(tt % 4) * 512 + 128], wtk(tt), eye_s[:]), inc=sT)
        sch.wait("v", sT, t_tr)
        wt_ev[tt] = sch.op("v", lambda e, tt=tt: e.tensor_copy(
            wT_s[0:8, tt * 128:(tt + 1) * 128],
            PS1[0:8, (tt % 4) * 512:(tt % 4) * 512 + 128]), inc=sV)
    v_wt = wt_ev[TT - 1]

    if phase_limit <= 4:
        return _finish_early()
    for cm in (cm_rts, cm_wtok, cm_scoS, cm_eye, cm_scr4,
               cm_chA, cm_rwbf, cm_rwB, cm_rwA, cm_rstdB, cm_meanB):
        cm.__exit__(None, None, None)
    sel_s, _p5s = alloc("sel_s", [8, 8 * 128])
    siluT, _p5t = alloc("siluT", [128, 2 * 512], BF16)
    gS, cm_gS = alloc("gS", [128, FT * TCW], BF16)
    rbB, cm_rbB = alloc("rbB", [128, 2 * TCW])
    hrb, cm_hrb = alloc("hrb", [128, 2 * KT * TCW], BF16)
    w13b, cm_w13 = alloc("w13b", [128, 2 * 2048], BF16)
    w2b, cm_w2 = alloc("w2b", [128, 2 * 2048], BF16)

    w13_ld, w2_ld = {}, {}
    g_ev = {}
    out_ev = {}
    t_g_done = {}
    t_out_done = {}
    sch.wait("g", sV, v_wt)
    sch.op("g", lambda e: e.dma_start(out=sel_s[:], in_=sel8_d.rearrange("k e m -> k (e m)")),
           inc=sGW, n=16)
    sel_ld = sGW.c

    hrb_ready = {}

    def emit_rb_hrb(tc, ei, wait_g_free=None, wait_ps_free=None):
        par = ei % 2
        sch.wait("t", sV, v_wt)
        sch.wait("t", sGW, sel_ld)
        if wait_ps_free is not None:
            sch.wait("t", sV, wait_ps_free)
        t_rb = None
        for c2 in range(NCC2):
            t_rb = sch.op("t", lambda e, ei=ei, tc=tc, c2=c2: e.matmul(
                PS0[:, c2 * 512: c2 * 512 + CCW],
                sel_s[0:8, ei * 128:(ei + 1) * 128],
                wT_s[0:8, tc * TCW + c2 * CCW: tc * TCW + c2 * CCW + CCW],
                start=True, stop=True), inc=sT)
        sch.wait("v", sT, t_rb)
        sch.drain("v")
        sch.op("v", lambda e, par=par: e.tensor_copy(
            rbB[:, par * TCW:(par + 1) * TCW], PS0[:, 0:TCW]))
        sch.drain("v")
        ev = None
        for k in range(KT):
            ev = sch.op("v", lambda e, k=k, tc=tc, par=par: e.tensor_tensor(
                hrb[:, par * KT * TCW + k * TCW: par * KT * TCW + (k + 1) * TCW],
                hffn[:, k * L + tc * TCW: k * L + tc * TCW + TCW],
                rbB[:, par * TCW:(par + 1) * TCW], OP.mult), inc=sV)
        hrb_ready[(tc, ei)] = ev
        return t_rb

    for tc in range(NTC):
        for ei in range(NE):
            par = ei % 2
            hbase = par * KT * TCW

            def pf_w13(f, ei=ei, tc=tc):
                prev = (tc, ei, f - 2)
                if f >= 2 and prev in t_g_done:
                    sch.wait("g", sT, t_g_done[prev])
                sch.op("g", lambda e, ei=ei, f=f: e.dma_start(
                    out=w13b[:, (f % 2) * 2048:(f % 2) * 2048 + 2048],
                    in_=w13_d[ei, f].rearrange("p a b c -> p (a b c)")), inc=sW13[f % 2], n=16)
                w13_ld[(tc, ei, f)] = (sW13[f % 2], sW13[f % 2].c)

            def pf_w2(do, ei=ei, tc=tc):
                prev = (tc, ei, do - 2)
                if do >= 2 and prev in t_out_done:
                    sch.wait("g", sT, t_out_done[prev])
                sch.op("g", lambda e, ei=ei, do=do: e.dma_start(
                    out=w2b[:, (do % 2) * 2048:(do % 2) * 2048 + 2048],
                    in_=w2t_d[ei, do].rearrange("p a b -> p (a b)")), inc=sW2[do % 2], n=16)
                w2_ld[(tc, ei, do)] = (sW2[do % 2], sW2[do % 2].c)

            pe = (tc, ei - 1) if ei > 0 else ((tc - 1, NE - 1) if tc > 0 else None)
            if pe is not None:
                sch.wait("g", sT, t_g_done[(pe[0], pe[1], FT - 1)])
            pf_w13(0)
            pf_w13(1)

            if (tc, ei) not in hrb_ready:
                wait_ps = None
                if pe is not None:
                    wait_ps = g_ev[(pe[0], pe[1], FT - 1, NCC2 - 1)]
                emit_rb_hrb(tc, ei, wait_ps_free=wait_ps)

            sch.wait("t", sV, hrb_ready[(tc, ei)])
            for f in range(FT):
                sch.wait("t", *w13_ld[(tc, ei, f)])
                tg = None
                for c2 in range(NCC2):
                    kp = (tc, ei, f - 1, c2) if f > 0 else \
                        ((tc, ei - 1, FT - 1, c2) if ei > 0 else
                         ((tc - 1, NE - 1, FT - 1, c2) if tc > 0 else None))
                    if kp is not None and kp in g_ev:
                        sch.wait("t", sV, g_ev[kp])
                    for k in range(KT):
                        sch.op("t", lambda e, f=f, c2=c2, k=k, tc=tc: e.matmul(
                            PS0[:, c2 * 512: c2 * 512 + CCW],
                            w13b[:, (f % 2) * 2048 + k * 128:(f % 2) * 2048 + k * 128 + 128],
                            hffn[:, k * L + tc * TCW + c2 * CCW: k * L + tc * TCW + c2 * CCW + CCW],
                            start=(k == 0), stop=(k == KT - 1)))
                    tg = None
                    for k in range(KT):
                        tg = sch.op("t", lambda e, f=f, c2=c2, k=k, hbase=hbase: e.matmul(
                            PS0[:, (NCC2 + c2) * 512:(NCC2 + c2) * 512 + CCW],
                            w13b[:, (f % 2) * 2048 + 1024 + k * 128:(f % 2) * 2048 + 1024 + k * 128 + 128],
                            hrb[:, hbase + k * TCW + c2 * CCW: hbase + k * TCW + c2 * CCW + CCW],
                            start=(k == 0), stop=(k == KT - 1)),
                            inc=sT if k == KT - 1 else None)
                    sch.wait("s", sT, tg)
                    spar = (f * NCC2 + c2) % 2
                    if g_ev:
                        sch.wait("s", sV, max(v for v in g_ev.values()))
                    sch.drain("s")
                    s_si = sch.op("s", lambda e, c2=c2, spar=spar: e.activation(
                        siluT[:, spar * 512: spar * 512 + CCW],
                        PS0[:, c2 * 512: c2 * 512 + CCW], AF.Silu), inc=sS)
                    sch.wait("v", sS, s_si)
                    g_ev[(tc, ei, f, c2)] = sch.op("v", lambda e, f=f, c2=c2, tc=tc, spar=spar: e.tensor_tensor(
                        gS[:, f * TCW + c2 * CCW: f * TCW + c2 * CCW + CCW],
                        siluT[:, spar * 512: spar * 512 + CCW],
                        PS0[:, (NCC2 + c2) * 512:(NCC2 + c2) * 512 + CCW], OP.mult), inc=sV)
                t_g_done[(tc, ei, f)] = tg
                if f + 2 < FT:
                    pf_w13(f + 2)
            nxt = (tc, ei + 1) if ei + 1 < NE else None
            if nxt is not None:
                emit_rb_hrb(nxt[0], nxt[1],
                            wait_ps_free=g_ev[(tc, ei, FT - 1, NCC2 - 1)])
            pf_w2(0)
            pf_w2(1)
            for do in range(KT):
                sch.wait("t", *w2_ld[(tc, ei, do)])
                keyp = (tc, ei, do - 2) if do >= 2 else \
                    ((tc, ei - 1, KT - 2 + do) if ei > 0 else
                     ((tc - 1, NE - 1, KT - 2 + do) if tc > 0 else None))
                if keyp is not None and keyp in out_ev:
                    sch.wait("t", sV, out_ev[keyp])
                sch.wait("t", sV, g_ev[(tc, ei, FT - 1, NCC2 - 1)])
                t_o = None
                for c2 in range(NCC2):
                    for fk in range(FT):
                        t_o = sch.op("t", lambda e, do=do, c2=c2, fk=fk: e.matmul(
                            PS1[:, (do % 2) * 1024 + c2 * 512:(do % 2) * 1024 + c2 * 512 + CCW],
                            w2b[:, (do % 2) * 2048 + fk * 128:(do % 2) * 2048 + fk * 128 + 128],
                            gS[:, fk * TCW + c2 * CCW: fk * TCW + c2 * CCW + CCW],
                            start=(fk == 0), stop=(fk == FT - 1)),
                            inc=sT if (c2 == NCC2 - 1 and fk == FT - 1) else None)
                t_out_done[(tc, ei, do)] = t_o
                if do + 2 < KT:
                    pf_w2(do + 2)
                sch.wait("v", sT, t_o)
                sch.drain("v")
                out_ev[(tc, ei, do)] = sch.op("v", lambda e, do=do, tc=tc: e.scalar_tensor_tensor(
                    x32[:, do * L + tc * TCW: do * L + tc * TCW + TCW],
                    PS1[:, (do % 2) * 1024:(do % 2) * 1024 + TCW],
                    col(modv, M_GBC + do),
                    x32[:, do * L + tc * TCW: do * L + tc * TCW + TCW],
                    OP.mult, OP.add), inc=sV)
        if phase_limit <= 7:
            return _finish_early()
        for ct in range(KT):
            sch.wait("y", sV, out_ev[(tc, NE - 1, ct)])
            sch.op("y", lambda e, ct=ct, tc=tc: e.dma_start(
                out=y_d[ct][:, tc * TCW:(tc + 1) * TCW],
                in_=x32[:, ct * L + tc * TCW: ct * L + tc * TCW + TCW]), inc=sO, n=16)

    sch.wait("y", sO, sO.c)

    return _replay()


_NC_CACHE = {}


def kernel(**inputs):
    n_cores = 8
    in_maps = host_prep(inputs, n_cores=n_cores, L=2048)
    if "nc" not in _NC_CACHE:
        _NC_CACHE["nc"] = build(L=2048)
    from concourse.bass_utils import run_bass_kernel_spmd
    res = run_bass_kernel_spmd(_NC_CACHE["nc"], in_maps, core_ids=list(range(n_cores)))
    return host_post(res.results)
